# version 06
# baseline (speedup 1.0000x reference)
import math
import contextlib
import numpy as np
import concourse.bass as bass
import concourse.mybir as mybir
from concourse.bass_utils import run_bass_kernel_spmd

F32 = mybir.dt.float32
BF16 = mybir.dt.bfloat16
AF = mybir.ActivationFunctionType
ALU = mybir.AluOpType
AX = mybir.AxisListType.X

B, T, E, H, D = 4, 4096, 2048, 16, 128
HPC = 8
EC = E // 128
TCH = T // 128
TGR = T // 512
MYT = 2048
MTC = MYT // 128
NEXP = 16
FCH = 11
EPS = 1e-5
NMASK = -60000.0
RSCALE = 1.0 / math.sqrt(D)

ENGINES = ("tensor", "vector", "scalar", "gpsimd", "sync")


class Sched:

    def __init__(self, nc, ctx):
        self.nc = nc
        self.sems = {}
        self.cnt = {}
        self.q = {e: [] for e in ENGINES}
        self.ctx = ctx

    def sem(self, name):
        h = self.ctx.enter_context(self.nc.semaphore(name))
        self.sems[name] = h
        self.cnt[name] = 0

    def op(self, eng, builder, waits=(), inc=None, inc_n=1):
        after = None
        waits = list(waits)
        if inc is not None:
            if inc_n == 16:
                waits.append((inc, self.cnt[inc]))
            self.cnt[inc] += inc_n
            after = self.cnt[inc]
        self.q[eng].append((tuple(waits), builder, inc, inc_n))
        return after

    def replay(self, block):
        for engname in ENGINES:
            ops = self.q[engname]
            if not ops:
                continue

            def body(eh, ops=ops):
                for waits, builder, inc, inc_n in ops:
                    for s, v in waits:
                        if v > 0:
                            eh.wait_ge(self.sems[s], v)
                    ins = builder(eh)
                    if inc is not None and ins is not None:
                        ins.then_inc(self.sems[inc], inc_n)

            getattr(block, engname)(body)
        self.q = {e: [] for e in ENGINES}


def build_program(n_exp=18, dump_scr=False, dump_ag=False):
    nc = bass.Bass("TRN2", target_bir_lowering=False, debug=False)
    dt = nc.dram_tensor

    xfull = dt("xfull", [T, E], F32, kind="ExternalInput").ap()
    xmine = dt("xmine", [MYT, E], F32, kind="ExternalInput").ap()
    par0 = dt("par0", [128, 1], F32, kind="ExternalInput").ap()
    par1 = dt("par1", [128, 1], F32, kind="ExternalInput").ap()
    wq_d = dt("wq", [EC, HPC, 128, 128], F32, kind="ExternalInput").ap()
    wk_d = dt("wk", [EC, HPC, 128, 128], F32, kind="ExternalInput").ap()
    wv_d = dt("wvT", [EC, 128, HPC * 128], F32, kind="ExternalInput").ap()
    wo_d = dt("woT", [EC, 128, E], F32, kind="ExternalInput").ap()
    bq_d = dt("bq_c", [128, HPC], F32, kind="ExternalInput").ap()
    bk_d = dt("bk_c", [128, HPC], F32, kind="ExternalInput").ap()
    bv_d = dt("bv_r", [1, HPC * 128], F32, kind="ExternalInput").ap()
    bo_d = dt("bo_r", [1, E], F32, kind="ExternalInput").ap()
    cos_d = dt("cosT", [128, T], F32, kind="ExternalInput").ap()
    sin_d = dt("sinS", [128, T], F32, kind="ExternalInput").ap()
    maskT_d = dt("maskT", [128, 128], F32, kind="ExternalInput").ap()
    identbf_d = dt("identbf", [128, 128], BF16, kind="ExternalInput").ap()
    gate_d = dt("gatewT", [EC, 128, NEXP], F32, kind="ExternalInput").ap()
    guw_d = dt("guw", [NEXP, FCH, 2, EC, 128, 128], F32, kind="ExternalInput").ap()
    dnw_d = dt("dnw", [NEXP, FCH, 128, E], F32, kind="ExternalInput").ap()
    sgu_d = dt("sguw", [2, FCH, 2, EC, 128, 128], F32, kind="ExternalInput").ap()
    sdn_d = dt("sdnw", [2, FCH, 128, E], F32, kind="ExternalInput").ap()
    out = dt("out", [MYT, E], F32, kind="ExternalOutput").ap()
    if dump_scr:
        dmp_k = dt("dmp_k", [HPC, 128, T], BF16, kind="ExternalOutput").ap()
        dmp_q = dt("dmp_q", [HPC, 128, T], BF16, kind="ExternalOutput").ap()
        dmp_v = dt("dmp_v", [HPC, TCH, 128, 128], BF16, kind="ExternalOutput").ap()
    if dump_ag:
        dmp_ag = dt("dmp_ag", [2 * HPC * 128, T], BF16, kind="ExternalOutput").ap()

    kscr = dt("kscr", [HPC, 128, T], BF16, kind="Internal").ap()
    qscr = dt("qscr", [HPC, 128, T], BF16, kind="Internal").ap()
    vscr = dt("vscr", [HPC, TCH, 128, 128], BF16, kind="Internal").ap()
    ag_i1 = dt("ag_i1", [HPC * 64, T], BF16, kind="Internal").ap()
    ag_i2 = dt("ag_i2", [HPC * 64, T], BF16, kind="Internal").ap()
    ag_o1 = dt("ag_o1", [HPC * 128, T], BF16, kind="Internal").ap()
    ag_o2 = dt("ag_o2", [HPC * 128, T], BF16, kind="Internal").ap()

    with contextlib.ExitStack() as ctx:
        S = Sched(nc, ctx)
        for s in ["s_x", "s_w", "s_wg", "s_tr", "s_pe", "s_ve", "s_ac", "s_gp", "s_kv",
                  "s_kvr", "s_fw", "s_dn", "s_cc", "s_y", "s_out", "s_acc", "s_xm", "s_dmp"]:
            S.sem(s)
        sbuf = lambda n, sh, d_: ctx.enter_context(nc.sbuf_tensor(n, sh, d_))

        ones128 = sbuf("ones128_s", [128, 128], F32)
        onesbf = sbuf("onesbf_s", [128, 128], BF16)
        maskT = sbuf("maskT_s", [128, 128], F32)
        identbf = sbuf("identbf_s", [128, 128], BF16)
        par0c = sbuf("par0c_s", [128, 1], F32)
        par1c = sbuf("par1c_s", [128, 1], F32)
        eps_t = sbuf("eps_t", [128, 1], F32)
        rowbuf = sbuf("rowbuf_s", [1, 1024], F32)

        block = ctx.enter_context(nc.Block())

        for apd, dst in [(maskT_d, maskT), (identbf_d, identbf),
                         (par0, par0c), (par1, par1c)]:
            S.op("sync", lambda sy, a=apd, d=dst: sy.dma_start(out=d[:], in_=a),
                 inc="s_w", inc_n=16)
        const_w = S.cnt["s_w"]
        S.op("vector", lambda v: v.memset(ones128[:], 1.0), inc="s_ve")
        S.op("vector", lambda v: v.memset(eps_t[:], EPS), inc="s_ve")
        init_ve = S.op("vector", lambda v: v.memset(onesbf[:], 1.0), inc="s_ve")

        with contextlib.ExitStack() as c1:
            sb1 = lambda n, sh, d_: c1.enter_context(nc.sbuf_tensor(n, sh, d_))
            ps1 = lambda n, sh: c1.enter_context(nc.psum_tensor(n, sh, F32))
            cosg = [sb1(f"cosg{i}", [128, 1024], F32) for i in range(2)]
            sing = [sb1(f"sing{i}", [128, 1024], F32) for i in range(2)]
            wq_sb = sb1("wq_sb", [128, EC * HPC * 128], BF16)
            wk_sb = sb1("wk_sb", [128, EC * HPC * 128], BF16)
            wv_sb = sb1("wv_sb", [128, EC * 1024], BF16)
            x_t = [sb1(f"x_t{i}", [128, E], F32) for i in range(2)]
            xln = [sb1(f"xln{i}", [128, E], BF16) for i in range(2)]
            xlnT = [sb1(f"xlnT{i}", [128, EC * 512], BF16) for i in range(2)]
            kst = [sb1(f"kst{i}", [128, 1024], F32) for i in range(2)]
            krot = [sb1(f"krot{i}", [128, 1024], F32) for i in range(2)]
            kbf = [sb1(f"kbf{i}", [128, 1024], BF16) for i in range(2)]
            vbf = [sb1(f"vbf{i}", [128, 1024], BF16) for i in range(2)]
            bqs = sb1("bqs", [128, HPC], F32)
            bks = sb1("bks", [128, HPC], F32)
            bvb = sb1("bvb", [128, 1024], F32)
            sqs = sb1("sqs", [128, 1024], F32)
            st1 = sb1("st1", [128, 8], F32)
            psP = [ps1(f"psP{i}", [128, 512]) for i in range(4)]
            psV = [ps1(f"psV{i}", [128, 512]) for i in range(2)]
            psTr = [c1.enter_context(nc.psum_tensor(f"psTr{i}", [128, 512], BF16))
                    for i in range(2)]

            S.op("sync", lambda sy: sy.dma_start(out=bqs[:], in_=bq_d), inc="s_w", inc_n=16)
            bk_w = S.op("sync", lambda sy: sy.dma_start(out=bks[:], in_=bk_d), inc="s_w", inc_n=16)
            S.op("sync", lambda sy: sy.dma_start(out=rowbuf[0:1, 0:1024], in_=bv_d),
                 inc="s_w", inc_n=16)
            for src_d, dst_sb in ((wq_d, wq_sb), (wk_d, wk_sb)):
                for eh in range(2):
                    S.op("gpsimd", lambda g, src_d=src_d, dst_sb=dst_sb, eh=eh: g.dma_start(
                        out=dst_sb[:, eh * 8192:(eh + 1) * 8192]
                        .rearrange("p (e f c) -> p e f c", e=EC // 2, f=HPC),
                        in_=src_d[eh * 8:(eh + 1) * 8].rearrange("e f p c -> p e f c")),
                        inc="s_wg", inc_n=16)
            S.op("gpsimd", lambda g: g.dma_start(out=wv_sb[:].rearrange("p (e c) -> p e c", e=EC), in_=wv_d.rearrange("e p c -> p e c")),
                 inc="s_wg", inc_n=16)
            w_done = S.cnt["s_w"]
            wg_done = S.cnt["s_wg"]
            bks_ve = S.op("vector", lambda v: v.tensor_scalar_mul(bks[:], bks[:], RSCALE),
                          waits=[("s_w", bk_w)], inc="s_ve")
            bvb_pe = 0
            for nn in range(2):
                bvb_pe = S.op("tensor", lambda t, nn=nn: t.matmul(
                    psV[nn][:], ones128[0:1, :],
                    rowbuf[0:1, nn * 512:(nn + 1) * 512], start=True, stop=True),
                    waits=[("s_w", w_done), ("s_ve", init_ve)], inc="s_pe")
            bvb_ve = 0
            for nn in range(2):
                bvb_ve = S.op("vector", lambda v, nn=nn: v.tensor_copy(
                    bvb[:, nn * 512:(nn + 1) * 512], psV[nn][:]),
                    waits=[("s_pe", bvb_pe)], inc="s_ve")

            ln_done = [0] * TCH
            trp_done = [0] * TCH
            trev_done = [0] * TCH
            pr_done = [0] * TGR
            evac_free = {}
            tr_bank_free = {}
            kq_c3 = {}
            kq_c5 = {}
            kbf_free = {}
            vbf_free = [0, 0]
            vps_free = {0: bvb_ve, 1: bvb_ve}
            kq_uses = 0
            rstd_prev = 0
            r4_prev = 0
            rope_read = [(0, 0), (0, 0)]
            cs_ld = [0] * TGR
            for tc in range(TCH):
                g = tc // 4
                if tc % 4 == 0:
                    rve, rgp = rope_read[g % 2]
                    waits = []
                    if rve:
                        waits += [("s_ve", rve), ("s_gp", rgp)]
                    for nn in range(2):
                        S.op("sync", lambda sy, g=g, nn=nn: sy.dma_start(
                            out=cosg[g % 2][:, nn * 512:(nn + 1) * 512],
                            in_=cos_d[:, g * 512:(g + 1) * 512]),
                            waits=waits if nn == 0 else (), inc="s_x", inc_n=16)
                        cs_ld[g] = S.op("sync", lambda sy, g=g, nn=nn: sy.dma_start(
                            out=sing[g % 2][:, nn * 512:(nn + 1) * 512],
                            in_=sin_d[:, g * 512:(g + 1) * 512]),
                            inc="s_x", inc_n=16)
                waits = [("s_ve", ln_done[tc - 2])] if tc >= 2 else []
                x_ld = S.op("sync", lambda sy, tc=tc: sy.dma_start(
                    out=x_t[tc % 2][:], in_=xfull[tc * 128:(tc + 1) * 128, :]),
                    waits=waits, inc="s_x", inc_n=16)
                waits = [("s_x", x_ld)]
                if r4_prev:
                    waits.append(("s_ve", r4_prev))
                S.op("scalar", lambda s, tc=tc: s.activation(
                    sqs[:], x_t[tc % 2][:, 0:1024], AF.Square, accum_out=st1[:, 4:5]),
                    waits=waits, inc="s_ac")
                sq2 = S.op("scalar", lambda s, tc=tc: s.activation(
                    sqs[:], x_t[tc % 2][:, 1024:2048], AF.Square, accum_out=st1[:, 5:6]),
                    inc="s_ac")
                r1 = S.op("vector", lambda v, tc=tc: v.reduce_sum(
                    st1[:, 0:1], x_t[tc % 2][:], AX),
                    waits=[("s_x", x_ld)], inc="s_ve")
                r2 = S.op("vector", lambda v: v.tensor_scalar_mul(st1[:, 0:1], st1[:, 0:1], 1.0 / E),
                          waits=[("s_ve", r1)], inc="s_ve")
                S.op("vector", lambda v: v.tensor_mul(st1[:, 1:2], st1[:, 0:1], st1[:, 0:1]),
                     waits=[("s_ve", r2)], inc="s_ve")
                waits = [("s_ac", sq2)]
                if rstd_prev:
                    waits.append(("s_ac", rstd_prev))
                r4_prev = S.op("vector", lambda v: v.tensor_add(
                    st1[:, 2:3], st1[:, 4:5], st1[:, 5:6]),
                    waits=waits, inc="s_ve")
                r4b = S.op("vector", lambda v: v.tensor_scalar_mul(
                    st1[:, 2:3], st1[:, 2:3], 1.0 / E),
                    waits=[("s_ve", r4_prev)], inc="s_ve")
                r5 = S.op("vector", lambda v: v.tensor_sub(
                    st1[:, 2:3], st1[:, 2:3], st1[:, 1:2]),
                    waits=[("s_ve", r4b)], inc="s_ve")
                a1 = S.op("scalar", lambda s: s.activation(
                    st1[:, 3:4], st1[:, 2:3], AF.Ln, bias=eps_t[:, 0:1]),
                    waits=[("s_ve", r5)], inc="s_ac")
                rstd_prev = S.op("scalar", lambda s: s.activation(
                    st1[:, 3:4], st1[:, 3:4], AF.Exp, scale=-0.5),
                    waits=[("s_ac", a1)], inc="s_ac")
                waits = [("s_ac", rstd_prev)]
                if tc >= 2:
                    waits.append(("s_pe", trp_done[tc - 2]))
                ln_done[tc] = S.op("vector", lambda v, tc=tc: v.tensor_scalar(
                    xln[tc % 2][:], x_t[tc % 2][:], st1[:, 0:1], st1[:, 3:4],
                    ALU.subtract, ALU.mult),
                    waits=waits, inc="s_ve")
                for e in range(EC):
                    b = (e // 4) % 2
                    waits = []
                    if e == 0:
                        waits.append(("s_ve", ln_done[tc]))
                    if e % 4 == 0 and tr_bank_free.get(b):
                        waits.append(("s_ac", tr_bank_free[b]))
                    trp_done[tc] = S.op("tensor", lambda t, tc=tc, e=e, b=b: t.transpose(
                        psTr[b][:, (e % 4) * 128:(e % 4 + 1) * 128],
                        xln[tc % 2][:, e * 128:(e + 1) * 128], identbf[:]),
                        waits=waits, inc="s_pe")
                    if e % 4 == 3:
                        waits = [("s_pe", trp_done[tc])]
                        if e == 3 and g >= 2:
                            waits.append(("s_pe", pr_done[g - 2]))
                        tr_bank_free[b] = S.op("scalar", lambda s, tc=tc, e=e, b=b, g=g:
                                               s.copy(xlnT[g % 2]
                                                      .rearrange("p (e c) -> p e c", e=EC)
                                                      [:, e - 3:e + 1, (tc % 4) * 128:(tc % 4) * 128 + 128],
                                                      psTr[b][:].rearrange("p (e c) -> p e c", e=4)),
                                               waits=waits, inc="s_ac")
                trev_done[tc] = S.cnt["s_ac"]

                if tc % 4 != 3:
                    continue

                xg = xlnT[g % 2]
                base_waits = [("s_ac", trev_done[tc])]
                if g == 0:
                    base_waits += [("s_w", w_done), ("s_wg", wg_done),
                                   ("s_ve", max(bvb_ve, bks_ve))]

                for h in range(HPC):
                    bankK = (2 * h) % 4
                    bankQ = (2 * h + 1) % 4
                    stg = h % 2
                    waits = list(base_waits if h == 0 else [])
                    if evac_free.get(bankK):
                        waits.append(("s_ac", evac_free[bankK]))

                    def mmk(t, h=h, bankK=bankK, xg=xg):
                        for e in range(EC):
                            mm = t.matmul(psP[bankK][:],
                                          wk_sb[:, (e * HPC + h) * 128:(e * HPC + h) * 128 + 128],
                                          xg[:, e * 512:(e + 1) * 512],
                                          start=(e == 0), stop=(e == EC - 1))
                        return mm
                    mmk_pe = S.op("tensor", mmk, waits=waits, inc="s_pe")
                    waits = []
                    if evac_free.get(bankQ):
                        waits.append(("s_ac", evac_free[bankQ]))

                    def mmq(t, h=h, bankQ=bankQ, xg=xg):
                        for e in range(EC):
                            mm = t.matmul(psP[bankQ][:],
                                          wq_sb[:, (e * HPC + h) * 128:(e * HPC + h) * 128 + 128],
                                          xg[:, e * 512:(e + 1) * 512],
                                          start=(e == 0), stop=(e == EC - 1))
                        return mm
                    mmq_pe = S.op("tensor", mmq, waits=waits, inc="s_pe")

                    waits = [("s_pe", mmk_pe)]
                    if kq_c3.get(stg):
                        waits.append(("s_ve", kq_c3[stg]))
                    evK = S.op("scalar", lambda s, bankK=bankK, stg=stg, h=h:
                               s.activation(kst[stg][:, 0:512], psP[bankK][:], AF.Identity,
                                            bias=bks[:, h:h + 1], scale=RSCALE),
                               waits=waits, inc="s_ac")
                    evac_free[bankK] = evK
                    evQ = S.op("scalar", lambda s, bankQ=bankQ, stg=stg, h=h:
                               s.activation(kst[stg][:, 512:1024], psP[bankQ][:], AF.Identity,
                                            bias=bqs[:, h:h + 1], scale=1.0),
                               waits=[("s_pe", mmq_pe)], inc="s_ac")
                    evac_free[bankQ] = evQ
                    waits = [("s_ac", evQ)]
                    if kq_c5.get(stg):
                        waits.append(("s_ve", kq_c5[stg]))
                    S.op("scalar", lambda s, stg=stg: s.copy(
                        krot[stg][0:64, :], kst[stg][64:128, :]), waits=waits, inc="s_ac")
                    c2 = S.op("scalar", lambda s, stg=stg: s.copy(
                        krot[stg][64:128, :], kst[stg][0:64, :]), inc="s_ac")
                    c3 = S.op("vector", lambda v, stg=stg, g=g: v.tensor_mul(
                        kst[stg][:], kst[stg][:], cosg[g % 2][:]),
                        waits=[("s_ac", c2), ("s_x", cs_ld[g])], inc="s_ve")
                    c4 = S.op("gpsimd", lambda gq, stg=stg, g=g: gq.tensor_mul(
                        krot[stg][:], krot[stg][:], sing[g % 2][:]),
                        waits=[("s_ac", c2), ("s_x", cs_ld[g])], inc="s_gp")
                    waits = [("s_ve", c3), ("s_gp", c4)]
                    if kbf_free.get(stg):
                        waits.append(("s_kv", kbf_free[stg]))
                    c5 = S.op("vector", lambda v, stg=stg: v.tensor_add(
                        kbf[stg][:], kst[stg][:], krot[stg][:]),
                        waits=waits, inc="s_ve")
                    kq_c3[stg] = c5
                    kq_c5[stg] = c5
                    S.op("gpsimd", lambda gq, h=h, g=g, stg=stg:
                         gq.dma_start(out=kscr[h, :, g * 512:(g + 1) * 512],
                                      in_=kbf[stg][:, 0:512]),
                         waits=[("s_ve", c5)], inc="s_kv", inc_n=16)
                    kbf_free[stg] = S.op("gpsimd", lambda gq, h=h, g=g, stg=stg:
                                         gq.dma_start(out=qscr[h, :, g * 512:(g + 1) * 512],
                                                      in_=kbf[stg][:, 512:1024]),
                                         inc="s_kv", inc_n=16)
                    kq_uses += 1
                rope_read[g % 2] = (c3, c4)

                for tsub in range(4):
                    tci = g * 4 + tsub
                    vmm = [0, 0]
                    for nn in range(2):
                        waits = list(base_waits if (tsub == 0 and nn == 0) else [])
                        if vps_free[nn]:
                            waits.append(("s_ve", vps_free[nn]))

                        def mmv(t, nn=nn, tsub=tsub, xg=xg):
                            for e in range(EC):
                                mm = t.matmul(psV[nn][:],
                                              xg[:, e * 512 + tsub * 128:e * 512 + tsub * 128 + 128],
                                              wv_sb[:, e * 1024 + nn * 512:e * 1024 + (nn + 1) * 512],
                                              start=(e == 0), stop=(e == EC - 1))
                            return mm
                        vmm[nn] = S.op("tensor", mmv, waits=waits, inc="s_pe")
                    vev = 0
                    for nn in range(2):
                        waits = [("s_pe", vmm[nn])]
                        if nn == 0 and vbf_free[tci % 2]:
                            waits.append(("s_kv", vbf_free[tci % 2]))
                        vev = S.op("vector", lambda v, nn=nn, tci=tci: v.tensor_add(
                            vbf[tci % 2][:, nn * 512:(nn + 1) * 512], psV[nn][:],
                            bvb[:, nn * 512:(nn + 1) * 512]), waits=waits, inc="s_ve")
                        vps_free[nn] = vev
                    vbf_free[tci % 2] = S.op("gpsimd", lambda gq, tci=tci: gq.dma_start(
                        out=vscr[:, tci, :, :].rearrange("h p c -> p h c"),
                        in_=vbf[tci % 2][:].rearrange("p (h c) -> p h c", h=HPC)),
                        waits=[("s_ve", vev)], inc="s_kv", inc_n=16)
                pr_done[g] = S.cnt["s_pe"]

            p1_kv = S.cnt["s_kv"]
            p1_tr = S.cnt["s_tr"]
            S.replay(block)

        with contextlib.ExitStack() as cB:

            with contextlib.ExitStack() as c2:
                sb2 = lambda n, sh, d_: c2.enter_context(nc.sbuf_tensor(n, sh, d_))
                ps2 = lambda n, sh: c2.enter_context(nc.psum_tensor(n, sh, F32))
                kT2 = [sb2(f"kT2_{i}", [128, T], BF16) for i in range(2)]
                qT2 = [sb2(f"qT2_{i}", [128, T], BF16) for i in range(2)]
                vt2 = [sb2(f"vt2_{i}", [128, T], BF16) for i in range(2)]
                y_sb = sb2("y_sb", [128, HPC * T], BF16)
                pT_bf = [sb2(f"pT_bf{i}", [128, 512], BF16) for i in range(4)]
                rlb = [sb2(f"rlb{i}", [128, 512], F32) for i in range(2)]
                psT = [ps2(f"psT{i}", [128, 512]) for i in range(4)]
                psO = [ps2(f"psO{i}", [128, 512]) for i in range(2)]
                psL = [ps2(f"psL{i}", [128, 512]) for i in range(2)]

                head_loads = [0] * HPC
                head_pe_done = [0] * HPC
                exp_free = {}
                pv_free = {}
                grp_epi = {0: 0, 1: 0}
                gidx = 0
                y_dma = [0] * HPC

                def emit_head_loads(h):
                    waits = [("s_kv", p1_kv)]
                    if h >= 2:
                        waits.append(("s_pe", head_pe_done[h - 2]))
                    S.op("sync", lambda sy, h=h: sy.dma_start(out=kT2[h % 2][:], in_=kscr[h]),
                         waits=waits, inc="s_kvr", inc_n=16)
                    S.op("sync", lambda sy, h=h: sy.dma_start(out=qT2[h % 2][:], in_=qscr[h]),
                         inc="s_kvr", inc_n=16)
                    head_loads[h] = S.op("sync", lambda sy, h=h: sy.dma_start(
                        out=vt2[h % 2][:].rearrange("p (a c) -> p a c", a=TCH), in_=vscr[h].rearrange("a p c -> p a c")),
                        inc="s_kvr", inc_n=16)

                emit_head_loads(0)
                for h in range(HPC):
                    if h + 1 < HPC:
                        emit_head_loads(h + 1)
                    kk, qq, vv = kT2[h % 2], qT2[h % 2], vt2[h % 2]

                    for g2 in range(TGR):
                        nkt = 4 * g2 + 4
                        pb = gidx % 2
                        gidx += 1
                        exps = [0] * nkt
                        lastL = [0]
                        lastV = [0]

                        def emit_LV(kt, g2=g2, nkt=nkt, pb=pb, vv=vv,
                                    exps=exps, lastL=lastL, lastV=lastV):
                            qoff = max(0, (kt - 4 * g2) * 128)
                            buf = kt % 4
                            waits = [("s_ac", exps[kt])]
                            if kt == 0 and grp_epi[pb]:
                                waits.append(("s_ve", grp_epi[pb]))
                            lastL[0] = S.op("tensor", lambda t, kt=kt, qoff=qoff, buf=buf:
                                            t.matmul(psL[pb][:, qoff:512], onesbf[:],
                                                     pT_bf[buf][:, qoff:512],
                                                     start=(kt == 0), stop=(kt == nkt - 1)),
                                            waits=waits, inc="s_pe")
                            lastV[0] = S.op("tensor", lambda t, kt=kt, qoff=qoff, buf=buf:
                                            t.matmul(psO[pb][:, qoff:512],
                                                     vv[:, kt * 128:(kt + 1) * 128],
                                                     pT_bf[buf][:, qoff:512],
                                                     start=(kt == 0), stop=(kt == nkt - 1)),
                                            inc="s_pe")
                            pv_free[buf] = lastV[0]

                        for kt in range(nkt):
                            qoff = max(0, (kt - 4 * g2) * 128)
                            bank = kt % 4
                            waits = [("s_kvr", head_loads[h])]
                            if exp_free.get(bank):
                                waits.append(("s_ac", exp_free[bank]))
                            mmS = S.op("tensor", lambda t, kt=kt, qoff=qoff, bank=bank,
                                       g2=g2, kk=kk, qq=qq: t.matmul(
                                           psT[bank][:, qoff:512],
                                           kk[:, kt * 128:(kt + 1) * 128],
                                           qq[:, g2 * 512 + qoff:(g2 + 1) * 512],
                                           start=True, stop=True),
                                       waits=waits, inc="s_pe")
                            dep = [("s_pe", mmS)]
                            if kt >= 4 * g2:
                                mk = S.op("vector", lambda v, bank=bank, qoff=qoff: v.tensor_add(
                                    psT[bank][:, qoff:qoff + 128],
                                    psT[bank][:, qoff:qoff + 128], maskT[:]),
                                    waits=dep, inc="s_ve")
                                dep = [("s_ve", mk)]
                            waits = list(dep)
                            if pv_free.get(bank):
                                waits.append(("s_pe", pv_free[bank]))
                            exps[kt] = S.op("scalar", lambda s, bank=bank, qoff=qoff:
                                            s.activation(pT_bf[bank][:, qoff:512],
                                                         psT[bank][:, qoff:512], AF.Exp),
                                            waits=waits, inc="s_ac")
                            exp_free[bank] = exps[kt]
                            if kt >= 2:
                                emit_LV(kt - 2)
                        emit_LV(nkt - 2)
                        emit_LV(nkt - 1)

                        waits = [("s_pe", lastL[0])]
                        if grp_epi[pb]:
                            waits.append(("s_ve", grp_epi[pb]))
                        rc = S.op("vector", lambda v, pb=pb: v.reciprocal(
                            rlb[pb][:], psL[pb][:]), waits=waits, inc="s_ve")
                        grp_epi[pb] = S.op("vector", lambda v, h=h, g2=g2, pb=pb: v.tensor_mul(
                            y_sb[:, h * T + g2 * 512:h * T + (g2 + 1) * 512],
                            psO[pb][:], rlb[pb][:]),
                            waits=[("s_pe", lastV[0]), ("s_ve", rc)], inc="s_ve")

                    head_pe_done[h] = S.cnt["s_pe"]
                    agt = ag_i1 if h < 4 else ag_i2
                    hr = h % 4
                    y_dma[h] = S.op("sync", lambda sy, h=h, agt=agt, hr=hr: sy.dma_start(
                        out=agt[hr * 128:(hr + 1) * 128, :], in_=y_sb[:, h * T:(h + 1) * T]),
                        waits=[("s_ve", grp_epi[(gidx - 1) % 2])], inc="s_y", inc_n=16)
                    if h == 3:
                        S.op("gpsimd", lambda g: g.collective_compute(
                            "AllGather", ALU.bypass,
                            replica_groups=[[0, 1], [2, 3], [4, 5], [6, 7]],
                            ins=[ag_i1.opt()], outs=[ag_o1.opt()]),
                            waits=[("s_y", y_dma[3])], inc="s_cc")

                cc_done = S.op("gpsimd", lambda g: g.collective_compute(
                    "AllGather", ALU.bypass,
                    replica_groups=[[0, 1], [2, 3], [4, 5], [6, 7]],
                    ins=[ag_i2.opt()], outs=[ag_o2.opt()]),
                    waits=[("s_y", y_dma[HPC - 1])], inc="s_cc")
                if dump_scr:
                    for src, dst in [(kscr, dmp_k), (qscr, dmp_q)]:
                        S.op("sync", lambda sy, s=src, d=dst: sy.dma_start(out=d, in_=s),
                             waits=[("s_kv", p1_kv)], inc="s_dmp", inc_n=16)
                    S.op("sync", lambda sy: sy.dma_start(out=dmp_v, in_=vscr),
                         waits=[("s_kv", p1_kv)], inc="s_dmp", inc_n=16)
                p2_ve = S.cnt["s_ve"]
                p2_ac = S.cnt["s_ac"]
                S.replay(block)

            with contextlib.ExitStack() as cC:
                sbC = lambda n, sh, d_: cC.enter_context(nc.sbuf_tensor(n, sh, d_))
                xln2T = sbC("xln2T", [128, EC * MYT], BF16)
                w_all = sbC("w_all", [128, MTC * NEXP], F32)
                gate_sb = sbC("gate_sb", [128, EC * NEXP], BF16)

                with contextlib.ExitStack() as c3:
                    sb3 = lambda n, sh, d_: c3.enter_context(nc.sbuf_tensor(n, sh, d_))
                    ps3 = lambda n, sh: c3.enter_context(nc.psum_tensor(n, sh, F32))
                    woT_sb = sb3("woT_sb", [128, EC * E], BF16)
                    bob = sb3("bob", [128, E], F32)
                    xm = sb3("xm", [128, E], F32)
                    y2ch = [sb3(f"y2ch{i}", [128, EC * 256], BF16) for i in range(2)]
                    y2bh = sb3("y2bh", [128, EC * 256], BF16)
                    x1 = [sb3(f"x1_{i}", [128, E], F32) for i in range(2)]
                    xln2 = [sb3(f"xln2_{i}", [128, E], BF16) for i in range(2)]
                    sqs3 = sb3("sqs3", [128, 1024], F32)
                    st3 = sb3("st3", [128, 8], F32)
                    sc4 = sb3("sc4", [128, 4 * NEXP], F32)
                    t4 = sb3("t4", [128, 4], F32)
                    psW = [ps3(f"psW{i}", [128, 512]) for i in range(2)]
                    psG = ps3("psG", [128, 512])
                    psTr3 = [c3.enter_context(nc.psum_tensor(f"psTr3_{i}", [128, 512], BF16))
                             for i in range(2)]

                    for _eng in ("tensor", "vector", "scalar"):
                        S.op(_eng, lambda e: None, waits=[("s_cc", cc_done)])
                    wo_w = S.op("gpsimd", lambda g: g.dma_start(
                        out=woT_sb[:].rearrange("p (e c) -> p e c", e=EC),
                        in_=wo_d.rearrange("e p c -> p e c")),
                        waits=[("s_y", S.cnt["s_y"]), ("s_ve", p2_ve), ("s_ac", p2_ac)],
                        inc="s_wg", inc_n=16)
                    gate_w = S.op("gpsimd", lambda g: g.dma_start(
                        out=gate_sb[:].rearrange("p (e c) -> p e c", e=EC), in_=gate_d.rearrange("e p c -> p e c")),
                        waits=[("s_cc", cc_done)], inc="s_wg", inc_n=16)
                    bob_ve = 0
                    for jj in range(2):
                        bo_w = S.op("sync", lambda sy, jj=jj: sy.dma_start(
                            out=rowbuf[:], in_=bo_d[0:1, jj * 1024:(jj + 1) * 1024]),
                            waits=[("s_cc", cc_done)] + ([("s_pe", bmm)] if jj else []),
                            inc="s_w", inc_n=16)
                        for j2 in range(2):
                            j = jj * 2 + j2
                            bmm = S.op("tensor", lambda t, j2=j2: t.matmul(
                                psW[0][:, 0:512], ones128[0:1, :],
                                rowbuf[0:1, j2 * 512:(j2 + 1) * 512], start=True, stop=True),
                                waits=[("s_w", bo_w)] + ([("s_ve", bob_ve)] if bob_ve else []),
                                inc="s_pe")
                            bob_ve = S.op("vector", lambda v, j=j: v.tensor_copy(
                                bob[:, j * 512:(j + 1) * 512], psW[0][:]),
                                waits=[("s_pe", bmm)], inc="s_ve")

                    x1_free = [0, 0]
                    x1_out = [0, 0]
                    wo_rd = [0, 0]
                    sel_rd = 0
                    xln2_free = [0, 0]
                    psW_free = {0: bob_ve, 1: 0}
                    tr3_bank_free = {}
                    xm_free = 0
                    tr3_ev = [0] * MTC
                    trp3 = [0] * MTC
                    psG_free = 0
                    out_init = 0
                    rstd3_prev = 0
                    r43_prev = 0

                    for ch in range(MTC // 2):
                        lda = ldb = 0
                        for fb in range(4):
                            src, r0 = [(ag_o1, 0), (ag_o2, 0),
                                       (ag_o1, 512), (ag_o2, 512)][fb]
                            waits = [("s_cc", 2 if ch == 0 else (1 if fb % 2 == 0 else 2))]
                            if fb == 0 and wo_rd[ch % 2]:
                                waits.append(("s_pe", wo_rd[ch % 2]))
                            lda = S.op("sync", lambda sy, ch=ch, fb=fb, src=src, r0=r0:
                                       sy.dma_start(
                                           out=y2ch[ch % 2][:, fb * 1024:(fb + 1) * 1024]
                                           .rearrange("p (f c) -> p f c", f=4),
                                           in_=src[r0:r0 + 512, ch * 256:(ch + 1) * 256]
                                           .rearrange("(f p) c -> p f c", p=128)),
                                       waits=waits, inc="s_kvr", inc_n=16)
                            waits = [("s_cc", 2 if ch == 0 else (1 if fb % 2 == 0 else 2))]
                            if fb == 0 and sel_rd:
                                waits.append(("s_ve", sel_rd))
                            ldb = S.op("scalar", lambda s, ch=ch, fb=fb, src=src, r0=r0:
                                       s.dma_start(
                                           out=y2bh[:, fb * 1024:(fb + 1) * 1024]
                                           .rearrange("p (f c) -> p f c", f=4),
                                           in_=src[r0:r0 + 512, MYT + ch * 256:MYT + (ch + 1) * 256]
                                           .rearrange("(f p) c -> p f c", p=128)),
                                       waits=waits, inc="s_tr", inc_n=16)
                        s1v = S.op("vector", lambda v, ch=ch: v.tensor_scalar(
                            y2ch[ch % 2][:], y2ch[ch % 2][:], par0c[:, 0:1], None, ALU.mult),
                            waits=[("s_kvr", lda)], inc="s_ve")
                        sel_done = S.op("vector", lambda v, ch=ch: v.scalar_tensor_tensor(
                            y2ch[ch % 2][:], y2bh[:], par1c[:, 0:1], y2ch[ch % 2][:],
                            ALU.mult, ALU.add),
                            waits=[("s_ve", s1v), ("s_tr", ldb)], inc="s_ve")
                        sel_rd = sel_done

                        for tci in range(2):
                            tc = 2 * ch + tci
                            waits = [("s_cc", cc_done)] if tc == 0 else []
                            if xm_free:
                                waits.append(("s_ve", xm_free))
                            xml = S.op("sync", lambda sy, tc=tc: sy.dma_start(
                                out=xm[:], in_=xmine[tc * 128:(tc + 1) * 128, :]),
                                waits=waits, inc="s_xm", inc_n=16)

                            wo_mm = 0
                            for oc in range(4):
                                waits = [("s_ve", sel_done)]
                                if oc == 0 and tc == 0:
                                    waits.append(("s_wg", wo_w))
                                if psW_free[oc % 2]:
                                    waits.append(("s_ve", psW_free[oc % 2]))

                                def mmw(t, ch=ch, tci=tci, oc=oc):
                                    for f in range(EC):
                                        mm = t.matmul(psW[oc % 2][:],
                                                      y2ch[ch % 2][:, f * 256 + tci * 128:f * 256 + tci * 128 + 128],
                                                      woT_sb[:, f * E + oc * 512:f * E + (oc + 1) * 512],
                                                      start=(f == 0), stop=(f == EC - 1))
                                    return mm
                                wo_mm = S.op("tensor", mmw, waits=waits, inc="s_pe")
                                waits = [("s_pe", wo_mm)]
                                if oc == 0:
                                    if x1_free[tc % 2]:
                                        waits.append(("s_ve", x1_free[tc % 2]))
                                    if x1_out[tc % 2]:
                                        waits.append(("s_out", x1_out[tc % 2]))
                                ev = S.op("vector", lambda v, tc=tc, oc=oc: v.scalar_tensor_tensor(
                                    x1[tc % 2][:, oc * 512:(oc + 1) * 512], psW[oc % 2][:], 1.0,
                                    bob[:, oc * 512:(oc + 1) * 512], ALU.mult, ALU.add),
                                    waits=waits, inc="s_ve")
                                psW_free[oc % 2] = ev
                            wo_rd[ch % 2] = wo_mm
                            xm_free = S.op("vector", lambda v, tc=tc: v.tensor_add(
                                x1[tc % 2][:], x1[tc % 2][:], xm[:]),
                                waits=[("s_ve", psW_free[1]), ("s_ve", psW_free[0]),
                                       ("s_xm", xml)], inc="s_ve")
                            x1_done = xm_free
                            x1_out[tc % 2] = S.op("gpsimd", lambda g, tc=tc: g.dma_start(
                                out=out[tc * 128:(tc + 1) * 128, :], in_=x1[tc % 2][:]),
                                waits=[("s_ve", x1_done)], inc="s_out", inc_n=16)

                            waits = [("s_ve", x1_done)]
                            if r43_prev:
                                waits.append(("s_ve", r43_prev))
                            S.op("scalar", lambda s, tc=tc: s.activation(
                                sqs3[:], x1[tc % 2][:, 0:1024], AF.Square,
                                accum_out=st3[:, 4:5]),
                                waits=waits, inc="s_ac")
                            sq2 = S.op("scalar", lambda s, tc=tc: s.activation(
                                sqs3[:], x1[tc % 2][:, 1024:2048], AF.Square,
                                accum_out=st3[:, 5:6]),
                                inc="s_ac")
                            r1 = S.op("vector", lambda v, tc=tc: v.reduce_sum(
                                st3[:, 0:1], x1[tc % 2][:], AX),
                                waits=[("s_ve", x1_done)], inc="s_ve")
                            r2 = S.op("vector", lambda v: v.tensor_scalar_mul(
                                st3[:, 0:1], st3[:, 0:1], 1.0 / E),
                                waits=[("s_ve", r1)], inc="s_ve")
                            S.op("vector", lambda v: v.tensor_mul(
                                st3[:, 1:2], st3[:, 0:1], st3[:, 0:1]),
                                waits=[("s_ve", r2)], inc="s_ve")
                            waits = [("s_ac", sq2)]
                            if rstd3_prev:
                                waits.append(("s_ac", rstd3_prev))
                            r43_prev = S.op("vector", lambda v: v.tensor_add(
                                st3[:, 2:3], st3[:, 4:5], st3[:, 5:6]),
                                waits=waits, inc="s_ve")
                            r4b = S.op("vector", lambda v: v.tensor_scalar_mul(
                                st3[:, 2:3], st3[:, 2:3], 1.0 / E),
                                waits=[("s_ve", r43_prev)], inc="s_ve")
                            r5 = S.op("vector", lambda v: v.tensor_sub(
                                st3[:, 2:3], st3[:, 2:3], st3[:, 1:2]),
                                waits=[("s_ve", r4b)], inc="s_ve")
                            a1 = S.op("scalar", lambda s: s.activation(
                                st3[:, 3:4], st3[:, 2:3], AF.Ln, bias=eps_t[:, 0:1]),
                                waits=[("s_ve", r5)], inc="s_ac")
                            rstd3_prev = S.op("scalar", lambda s: s.activation(
                                st3[:, 3:4], st3[:, 3:4], AF.Exp, scale=-0.5),
                                waits=[("s_ac", a1)], inc="s_ac")
                            waits = [("s_ac", rstd3_prev)]
                            if xln2_free[tc % 2]:
                                waits.append(("s_pe", xln2_free[tc % 2]))
                            ln2_ap = S.op("vector", lambda v, tc=tc: v.tensor_scalar(
                                xln2[tc % 2][:], x1[tc % 2][:], st3[:, 0:1], st3[:, 3:4],
                                ALU.subtract, ALU.mult),
                                waits=waits, inc="s_ve")
                            x1_free[tc % 2] = ln2_ap
                            for e in range(EC):
                                b = (e // 4) % 2
                                waits = []
                                if e == 0:
                                    waits.append(("s_ve", ln2_ap))
                                if e % 4 == 0 and tr3_bank_free.get(b):
                                    waits.append(("s_ac", tr3_bank_free[b]))
                                trp3[tc] = S.op("tensor", lambda t, tc=tc, e=e, b=b: t.transpose(
                                    psTr3[b][:, (e % 4) * 128:(e % 4 + 1) * 128],
                                    xln2[tc % 2][:, e * 128:(e + 1) * 128], identbf[:]),
                                    waits=waits, inc="s_pe")
                                if e % 4 == 3:
                                    tr3_bank_free[b] = S.op(
                                        "scalar", lambda s, tc=tc, e=e, b=b: s.copy(
                                            xln2T.rearrange("p (e c) -> p e c", e=EC)
                                            [:, e - 3:e + 1, tc * 128:(tc + 1) * 128],
                                            psTr3[b][:].rearrange("p (e c) -> p e c", e=4)),
                                        waits=[("s_pe", trp3[tc])], inc="s_ac")
                            tr3_ev[tc] = S.cnt["s_ac"]
                            xln2_free[tc % 2] = trp3[tc]

                            waits = [("s_ac", tr3_ev[tc]), ("s_wg", gate_w)]
                            if psG_free:
                                waits.append(("s_ve", psG_free))

                            def mmg(t, tc=tc):
                                for e in range(EC):
                                    mm = t.matmul(psG[:, 0:NEXP],
                                                  xln2T[:, e * MYT + tc * 128:e * MYT + (tc + 1) * 128],
                                                  gate_sb[:, e * NEXP:(e + 1) * NEXP],
                                                  start=(e == 0), stop=(e == EC - 1))
                                return mm
                            gmm = S.op("tensor", mmg, waits=waits, inc="s_pe")
                            sc = sc4
                            g0 = S.op("vector", lambda v: v.tensor_copy(sc[:, 0:16], psG[:, 0:16]),
                                      waits=[("s_pe", gmm)], inc="s_ve")
                            psG_free = g0
                            g1 = S.op("vector", lambda v: v.tensor_reduce(
                                t4[:, 0:1], sc[:, 0:16], AX, ALU.max, negate=True),
                                waits=[("s_ve", g0)], inc="s_ve")
                            g2x = S.op("scalar", lambda s: s.activation(
                                sc[:, 16:32], sc[:, 0:16], AF.Exp, bias=t4[:, 0:1],
                                accum_out=t4[:, 1:2]),
                                waits=[("s_ve", g1)], inc="s_ac")
                            g3 = S.op("vector", lambda v: v.reciprocal(t4[:, 2:3], t4[:, 1:2]),
                                      waits=[("s_ac", g2x)], inc="s_ve")
                            g4 = S.op("vector", lambda v: v.tensor_scalar(
                                sc[:, 32:48], sc[:, 16:32], t4[:, 2:3], None, ALU.mult),
                                waits=[("s_ve", g3)], inc="s_ve")
                            g5 = S.op("vector", lambda v: v.tensor_copy(sc[:, 48:64], sc[:, 32:48]),
                                      waits=[("s_ve", g4)], inc="s_ve")
                            prev = g5
                            for it in range(4):
                                m1 = S.op("vector", lambda v: v.tensor_reduce(
                                    t4[:, 3:4], sc[:, 48:64], AX, ALU.max),
                                    waits=[("s_ve", prev)], inc="s_ve")
                                if it < 3:
                                    e1 = S.op("vector", lambda v: v.tensor_scalar(
                                        sc[:, 16:32], sc[:, 48:64], t4[:, 3:4], None, ALU.is_ge),
                                        waits=[("s_ve", m1)], inc="s_ve")
                                    prev = S.op("vector", lambda v: v.scalar_tensor_tensor(
                                        sc[:, 48:64], sc[:, 16:32], -1e30, sc[:, 48:64],
                                        ALU.mult, ALU.add),
                                        waits=[("s_ve", e1)], inc="s_ve")
                                else:
                                    prev = m1
                            ge4 = S.op("vector", lambda v: v.tensor_scalar(
                                sc[:, 16:32], sc[:, 32:48], t4[:, 3:4], None, ALU.is_ge),
                                waits=[("s_ve", prev)], inc="s_ve")
                            S.op("vector", lambda v, tc=tc: v.tensor_mul(
                                w_all[:, tc * NEXP:(tc + 1) * NEXP], sc[:, 32:48], sc[:, 16:32]),
                                waits=[("s_ve", ge4)], inc="s_ve")

                    out_init = S.cnt["s_out"]
                    p3_tr = S.cnt["s_tr"]
                    p4_ve = S.cnt["s_ve"]
                    p4_ac = S.cnt["s_ac"]
                    S.replay(block)

                with contextlib.ExitStack() as c5:
                    sb5 = lambda n, sh, d_: c5.enter_context(nc.sbuf_tensor(n, sh, d_))
                    ps5 = lambda n, sh: c5.enter_context(nc.psum_tensor(n, sh, F32))
                    dn_sb = sb5("dn_sb", [128, FCH * E], BF16)
                    h_sb = sb5("h_sb", [128, FCH * MYT], BF16)
                    gu_ring = [sb5(f"gur{i}", [128, 2 * EC * 128], BF16) for i in range(2)]
                    ystage = [sb5(f"yst{i}", [128, E], F32) for i in range(2)]
                    sg = [sb5(f"sg{i}", [128, 512], BF16) for i in range(4)]
                    psGU = [ps5(f"psGU{i}", [128, 512]) for i in range(4)]
                    psDN = [ps5(f"psDN{i}", [128, 512]) for i in range(4)]

                    for _eng in ("tensor", "vector", "scalar"):
                        S.op(_eng, lambda e: None,
                             waits=[("s_out", out_init), ("s_tr", p3_tr),
                                    ("s_ac", p4_ac), ("s_xm", S.cnt["s_xm"]),
                                    ("s_kvr", S.cnt["s_kvr"])])
                    dn_free = 0
                    ring_free = [0, 0]
                    gu_free = {}
                    dnps_free = {}
                    sg_free = [0, 0, 0, 0]
                    yst_free = [0, 0]
                    gu_uses = 0

                    p5_fence = [("s_out", out_init), ("s_ve", p4_ve),
                                ("s_tr", p3_tr), ("s_ac", p4_ac),
                                ("s_xm", S.cnt["s_xm"]), ("s_kvr", S.cnt["s_kvr"])]
                    pair_recs = {}

                    def gsrc_of(xp2):
                        return guw_d[xp2] if xp2 < NEXP else sgu_d[xp2 - NEXP]

                    def emit_pair(xp2, fi, extra=()):
                        slot = (xp2 * FCH + fi) % 2
                        waits = list(extra)
                        if ring_free[slot]:
                            waits.append(("s_pe", ring_free[slot]))
                        pair_recs[(xp2, fi)] = S.op(
                            "gpsimd", lambda g, xp2=xp2, fi=fi, slot=slot: g.dma_start(
                                out=gu_ring[slot][:].rearrange("p (a e c) -> p a e c", a=2, e=EC),
                                in_=gsrc_of(xp2)[fi].rearrange("a e p c -> p a e c")),
                            waits=waits, inc="s_fw", inc_n=16)

                    emit_pair(0, 0, p5_fence)
                    emit_pair(0, 1, p5_fence)
                    for xp in range(n_exp):
                        acc_base = S.cnt["s_acc"]
                        shared = xp >= NEXP
                        dsrc = dnw_d[xp] if not shared else sdn_d[xp - NEXP]
                        waits = [("s_out", out_init)]
                        if xp == 0:
                            waits = list(p5_fence)
                        if dn_free:
                            waits.append(("s_pe", dn_free))
                        dn_rec = S.op("gpsimd", lambda g, dsrc=dsrc: g.dma_start(
                            out=dn_sb[:].rearrange("p (f c) -> p f c", f=FCH), in_=dsrc.rearrange("f p c -> p f c")),
                            waits=waits, inc="s_dn", inc_n=16)

                        h_writes = 0
                        for fi in range(FCH):
                            slot = (xp * FCH + fi) % 2
                            pair_rec = pair_recs[(xp, fi)]
                            last_mm = 0
                            sils = [0] * 4
                            for half in range(2):
                                mm_last = [0] * 4
                                for e in range(EC):
                                    for tg in range(4):
                                        waits = []
                                        if e == 0:
                                            if tg == 0:
                                                waits.append(("s_fw", pair_rec))
                                                if fi == 0 and xp == 0 and half == 0:
                                                    waits.append(("s_ac", p4_ac))
                                            if gu_free.get(tg):
                                                waits.append(gu_free[tg])
                                        mm_last[tg] = S.op(
                                            "tensor", lambda t, slot=slot, half=half,
                                            e=e, tg=tg: t.matmul(
                                                psGU[tg][:],
                                                gu_ring[slot][:, half * 2048 + e * 128:half * 2048 + (e + 1) * 128],
                                                xln2T[:, e * MYT + tg * 512:e * MYT + (tg + 1) * 512],
                                                start=(e == 0), stop=(e == EC - 1)),
                                            waits=waits, inc="s_pe")
                                last_mm = mm_last[3]
                                if half == 0:
                                    for tg in range(4):
                                        waits = [("s_pe", mm_last[tg])]
                                        if sg_free[tg]:
                                            waits.append(("s_ve", sg_free[tg]))
                                        sils[tg] = S.op("scalar", lambda s, tg=tg: s.activation(
                                            sg[tg][:], psGU[tg][:], AF.Silu),
                                            waits=waits, inc="s_ac")
                                        gu_free[tg] = ("s_ac", sils[tg])
                                else:
                                    for tg in range(4):
                                        hv = S.op("vector", lambda v, fi=fi, tg=tg: v.tensor_mul(
                                            h_sb[:, fi * MYT + tg * 512:fi * MYT + (tg + 1) * 512],
                                            sg[tg][:], psGU[tg][:]),
                                            waits=[("s_ac", sils[tg]), ("s_pe", mm_last[tg])],
                                            inc="s_ve")
                                        gu_free[tg] = ("s_ve", hv)
                                        sg_free[tg] = hv
                                        h_writes = hv
                            ring_free[slot] = last_mm
                            if fi + 2 < FCH:
                                emit_pair(xp, fi + 2)

                        if xp + 1 < n_exp:
                            emit_pair(xp + 1, 0)
                            emit_pair(xp + 1, 1)

                        dn_last = 0
                        for tc2 in range(MTC):
                            mm_last = [0] * 4
                            for fi in range(FCH):
                                for oc in range(4):
                                    waits = []
                                    if fi == 0:
                                        if oc == 0:
                                            waits += [("s_ve", h_writes), ("s_dn", dn_rec)]
                                        if dnps_free.get(oc):
                                            waits.append(("s_ve", dnps_free[oc]))
                                    mm_last[oc] = S.op(
                                        "tensor", lambda t, tc2=tc2, fi=fi, oc=oc: t.matmul(
                                            psDN[oc][:],
                                            h_sb[:, fi * MYT + tc2 * 128:fi * MYT + (tc2 + 1) * 128],
                                            dn_sb[:, fi * E + oc * 512:fi * E + (oc + 1) * 512],
                                            start=(fi == 0), stop=(fi == FCH - 1)),
                                        waits=waits, inc="s_pe")
                            dn_last = mm_last[3]
                            for oc in range(4):
                                waits = [("s_pe", mm_last[oc])]
                                if oc == 0 and yst_free[tc2 % 2]:
                                    waits.append(("s_acc", yst_free[tc2 % 2]))
                                if not shared:
                                    ystv = S.op("vector", lambda v, tc2=tc2, oc=oc, xp=xp:
                                                v.tensor_scalar(
                                                    ystage[tc2 % 2][:, oc * 512:(oc + 1) * 512],
                                                    psDN[oc][:],
                                                    w_all[:, tc2 * NEXP + xp:tc2 * NEXP + xp + 1],
                                                    None, ALU.mult),
                                                waits=waits, inc="s_ve")
                                else:
                                    ystv = S.op("vector", lambda v, tc2=tc2, oc=oc:
                                                v.tensor_copy(ystage[tc2 % 2][:, oc * 512:(oc + 1) * 512],
                                                              psDN[oc][:]),
                                                waits=waits, inc="s_ve")
                                dnps_free[oc] = ystv
                            yst_free[tc2 % 2] = S.op("gpsimd", lambda g, tc2=tc2: g.dma_start(
                                out=out[tc2 * 128:(tc2 + 1) * 128, :], in_=ystage[tc2 % 2][:],
                                accum_op=ALU.add),
                                waits=[("s_ve", S.cnt["s_ve"]),
                                       ("s_out", out_init), ("s_acc", acc_base)],
                                inc="s_acc", inc_n=16)
                        dn_free = dn_last

                    S.op("gpsimd", lambda g: None,
                         waits=[("s_acc", S.cnt["s_acc"]), ("s_out", S.cnt["s_out"]),
                                ("s_dmp", S.cnt["s_dmp"])])
                    S.replay(block)

    return nc


def _host_prep(L0_attn_x, L0_attn_Wq, L0_attn_bq, L0_attn_Wk, L0_attn_bk,
               L0_attn_Wv, L0_attn_bv, L0_attn_Wo, L0_attn_bo,
               L0_ffn_gate_weight, L0_ffn_gate_up_weights, L0_ffn_down_weights,
               L0_ffn_shared_gate_up, L0_ffn_shared_down):
    C = np.ascontiguousarray
    x = np.asarray(L0_attn_x, dtype=np.float32)
    Wq = np.asarray(L0_attn_Wq, dtype=np.float32)
    Wk = np.asarray(L0_attn_Wk, dtype=np.float32)
    Wv = np.asarray(L0_attn_Wv, dtype=np.float32)
    Wo = np.asarray(L0_attn_Wo, dtype=np.float32)
    bq = np.asarray(L0_attn_bq, dtype=np.float32)
    bk = np.asarray(L0_attn_bk, dtype=np.float32)
    bv = np.asarray(L0_attn_bv, dtype=np.float32)
    bo = np.asarray(L0_attn_bo, dtype=np.float32)
    gw = np.asarray(L0_ffn_gate_weight, dtype=np.float32)
    guw = np.asarray(L0_ffn_gate_up_weights, dtype=np.float32)
    dnw = np.asarray(L0_ffn_down_weights, dtype=np.float32)
    sgu = np.asarray(L0_ffn_shared_gate_up, dtype=np.float32)
    sdn = np.asarray(L0_ffn_shared_down, dtype=np.float32)

    j = np.arange(64, dtype=np.float64)
    theta = 10000.0 ** (-2.0 * j / D)
    tpos = np.arange(1, T + 1, dtype=np.float64)
    ang = tpos[None, :] * np.concatenate([theta, theta])[:, None]
    cosT = C(np.cos(ang).astype(np.float32))
    sinS = np.sin(ang).astype(np.float32)
    sinS[0:64, :] *= -1.0
    sinS = C(sinS)
    ii = np.arange(128)
    maskT = C(np.where(ii[:, None] <= ii[None, :], 0.0, NMASK).astype(np.float32))
    import ml_dtypes
    identbf = np.eye(128, dtype=np.float32).astype(mybir.dt.np(BF16))

    per_par = {}
    for par in range(2):
        hsl = slice(par * HPC * 128, (par + 1) * HPC * 128)
        wq_t = C(Wq.T[:, hsl].reshape(EC, 128, HPC, 128).transpose(0, 2, 1, 3))
        wk_t = C(Wk.T[:, hsl].reshape(EC, 128, HPC, 128).transpose(0, 2, 1, 3))
        wv_t = C(Wv.T[:, hsl].reshape(EC, 128, HPC * 128))
        bq_c = C(bq[hsl].reshape(HPC, 128).T)
        bk_c = C(bk[hsl].reshape(HPC, 128).T)
        bv_r = C(bv[hsl][None, :])
        per_par[par] = dict(wq=wq_t, wk=wk_t, wvT=wv_t, bq_c=bq_c, bk_c=bk_c, bv_r=bv_r,
                            par0=np.full((128, 1), 1.0 - par, np.float32),
                            par1=np.full((128, 1), float(par), np.float32))

    woT = C(Wo.T.reshape(EC, 128, E))
    bo_r = C(bo[None, :])
    gatewT = C(gw.T.reshape(EC, 128, NEXP))
    guw_t = C(guw.reshape(NEXP, 2, FCH, 128, EC, 128).transpose(0, 2, 1, 4, 5, 3))
    dnw_t = C(dnw.transpose(0, 2, 1).reshape(NEXP, FCH, 128, E))
    g_all = sgu[0:2816]
    u_all = sgu[2816:]
    sgu_t = np.empty((2, FCH, 2, EC, 128, 128), np.float32)
    for X in range(2):
        for fi in range(FCH):
            gt = g_all[(X * FCH + fi) * 128:(X * FCH + fi + 1) * 128]
            ut = u_all[(X * FCH + fi) * 128:(X * FCH + fi + 1) * 128]
            sgu_t[X, fi, 0] = gt.T.reshape(EC, 128, 128)
            sgu_t[X, fi, 1] = ut.T.reshape(EC, 128, 128)
    sgu_t = C(sgu_t)
    sdn_t = C(sdn.T.reshape(2, FCH, 128, E))

    shared_all = dict(woT=woT, bo_r=bo_r, cosT=cosT, sinS=sinS,
                      maskT=maskT, identbf=identbf, gatewT=gatewT, guw=guw_t,
                      dnw=dnw_t, sguw=sgu_t, sdnw=sdn_t)
    in_maps = []
    for core in range(8):
        b = core // 2
        par = core % 2
        m = dict(shared_all)
        m.update(per_par[par])
        m["xfull"] = C(x[b])
        m["xmine"] = C(x[b, par * MYT:(par + 1) * MYT])
        in_maps.append(m)
    return in_maps, x.dtype


def kernel(**inputs):
    in_maps, xdt = _host_prep(**inputs)
    nc = build_program()
    res = run_bass_kernel_spmd(nc, in_maps, core_ids=list(range(8)))
    outf = np.empty((B, T, E), dtype=np.float32)
    for core in range(8):
        b, par = core // 2, core % 2
        outf[b, par * MYT:(par + 1) * MYT] = res.results[core]["out"]
    return outf


# version 07
# speedup vs baseline: 1.1034x; 1.1034x over previous
import math
import contextlib
import numpy as np
import concourse.bass as bass
import concourse.mybir as mybir
from concourse.bass_utils import run_bass_kernel_spmd

F32 = mybir.dt.float32
BF16 = mybir.dt.bfloat16
AF = mybir.ActivationFunctionType
ALU = mybir.AluOpType
AX = mybir.AxisListType.X

B, T, E, H, D = 4, 4096, 2048, 16, 128
HPC = 8
EC = E // 128
TCH = T // 128
TGR = T // 512
MYT = 2048
MTC = MYT // 128
NEXP = 16
FCH = 11
EPS = 1e-5
NMASK = -60000.0
RSCALE = 1.0 / math.sqrt(D)

ENGINES = ("tensor", "vector", "scalar", "gpsimd", "sync")


class Sched:

    def __init__(self, nc, ctx):
        self.nc = nc
        self.sems = {}
        self.cnt = {}
        self.q = {e: [] for e in ENGINES}
        self.ctx = ctx

    def sem(self, name):
        h = self.ctx.enter_context(self.nc.semaphore(name))
        self.sems[name] = h
        self.cnt[name] = 0

    def op(self, eng, builder, waits=(), inc=None, inc_n=1):
        after = None
        waits = list(waits)
        if inc is not None:
            if inc_n == 16:
                waits.append((inc, self.cnt[inc]))
            self.cnt[inc] += inc_n
            after = self.cnt[inc]
        self.q[eng].append((tuple(waits), builder, inc, inc_n))
        return after

    def replay(self, block):
        for engname in ENGINES:
            ops = self.q[engname]
            if not ops:
                continue

            def body(eh, ops=ops):
                for waits, builder, inc, inc_n in ops:
                    for s, v in waits:
                        if v > 0:
                            eh.wait_ge(self.sems[s], v)
                    ins = builder(eh)
                    if inc is not None and ins is not None:
                        ins.then_inc(self.sems[inc], inc_n)

            getattr(block, engname)(body)
        self.q = {e: [] for e in ENGINES}


def build_program(n_exp=18, dump_scr=False, dump_ag=False):
    nc = bass.Bass("TRN2", target_bir_lowering=False, debug=False)
    dt = nc.dram_tensor

    xfull = dt("xfull", [T, E], F32, kind="ExternalInput").ap()
    xmine = dt("xmine", [MYT, E], F32, kind="ExternalInput").ap()
    par0 = dt("par0", [128, 1], F32, kind="ExternalInput").ap()
    par1 = dt("par1", [128, 1], F32, kind="ExternalInput").ap()
    wq_d = dt("wq", [128, EC, HPC, 128], F32, kind="ExternalInput").ap()
    wk_d = dt("wk", [128, EC, HPC, 128], F32, kind="ExternalInput").ap()
    wv_d = dt("wvT", [128, EC, HPC * 128], F32, kind="ExternalInput").ap()
    wo_d = dt("woT", [EC, 128, E], F32, kind="ExternalInput").ap()
    bq_d = dt("bq_c", [128, HPC], F32, kind="ExternalInput").ap()
    bk_d = dt("bk_c", [128, HPC], F32, kind="ExternalInput").ap()
    bv_d = dt("bv_r", [1, HPC * 128], F32, kind="ExternalInput").ap()
    bo_d = dt("bo_r", [1, E], F32, kind="ExternalInput").ap()
    cos_d = dt("cosT", [128, T], F32, kind="ExternalInput").ap()
    sin_d = dt("sinS", [128, T], F32, kind="ExternalInput").ap()
    maskT_d = dt("maskT", [128, 128], F32, kind="ExternalInput").ap()
    identbf_d = dt("identbf", [128, 128], BF16, kind="ExternalInput").ap()
    gate_d = dt("gatewT", [128, EC, NEXP], F32, kind="ExternalInput").ap()
    guw_d = dt("guw", [NEXP, FCH, 128, 2, EC, 128], F32, kind="ExternalInput").ap()
    dnw_d = dt("dnw", [NEXP, FCH, 128, E], F32, kind="ExternalInput").ap()
    sgu_d = dt("sguw", [2, FCH, 128, 2, EC, 128], F32, kind="ExternalInput").ap()
    sdn_d = dt("sdnw", [2, FCH, 128, E], F32, kind="ExternalInput").ap()
    out = dt("out", [MYT, E], F32, kind="ExternalOutput").ap()
    if dump_scr:
        dmp_k = dt("dmp_k", [HPC, 128, T], BF16, kind="ExternalOutput").ap()
        dmp_q = dt("dmp_q", [HPC, 128, T], BF16, kind="ExternalOutput").ap()
        dmp_v = dt("dmp_v", [HPC, TCH, 128, 128], BF16, kind="ExternalOutput").ap()
    if dump_ag:
        dmp_ag = dt("dmp_ag", [2 * HPC * 128, T], BF16, kind="ExternalOutput").ap()

    kscr = dt("kscr", [HPC, 128, T], BF16, kind="Internal").ap()
    qscr = dt("qscr", [HPC, 128, T], BF16, kind="Internal").ap()
    vscr = dt("vscr", [TCH, 128, HPC * 128], BF16, kind="Internal").ap()
    ag_i1 = dt("ag_i1", [HPC * 64, T], BF16, kind="Internal").ap()
    ag_i2 = dt("ag_i2", [HPC * 64, T], BF16, kind="Internal").ap()
    ag_o1 = dt("ag_o1", [HPC * 128, T], BF16, kind="Internal").ap()
    ag_o2 = dt("ag_o2", [HPC * 128, T], BF16, kind="Internal").ap()

    with contextlib.ExitStack() as ctx:
        S = Sched(nc, ctx)
        for s in ["s_x", "s_w", "s_wg", "s_tr", "s_pe", "s_ve", "s_ac", "s_gp", "s_kv",
                  "s_kvr", "s_fw", "s_dn", "s_cc", "s_y", "s_out", "s_acc", "s_xm", "s_dmp"]:
            S.sem(s)
        sbuf = lambda n, sh, d_: ctx.enter_context(nc.sbuf_tensor(n, sh, d_))

        ones128 = sbuf("ones128_s", [128, 128], F32)
        onesbf = sbuf("onesbf_s", [128, 128], BF16)
        maskT = sbuf("maskT_s", [128, 128], F32)
        identbf = sbuf("identbf_s", [128, 128], BF16)
        par0c = sbuf("par0c_s", [128, 1], F32)
        par1c = sbuf("par1c_s", [128, 1], F32)
        eps_t = sbuf("eps_t", [128, 1], F32)
        rowbuf = sbuf("rowbuf_s", [1, 1024], F32)

        block = ctx.enter_context(nc.Block())

        for apd, dst in [(maskT_d, maskT), (identbf_d, identbf),
                         (par0, par0c), (par1, par1c)]:
            S.op("sync", lambda sy, a=apd, d=dst: sy.dma_start(out=d[:], in_=a),
                 inc="s_w", inc_n=16)
        const_w = S.cnt["s_w"]
        S.op("vector", lambda v: v.memset(ones128[:], 1.0), inc="s_ve")
        S.op("vector", lambda v: v.memset(eps_t[:], EPS), inc="s_ve")
        init_ve = S.op("vector", lambda v: v.memset(onesbf[:], 1.0), inc="s_ve")

        with contextlib.ExitStack() as c1:
            sb1 = lambda n, sh, d_: c1.enter_context(nc.sbuf_tensor(n, sh, d_))
            ps1 = lambda n, sh: c1.enter_context(nc.psum_tensor(n, sh, F32))
            cosg = [sb1(f"cosg{i}", [128, 1024], F32) for i in range(2)]
            sing = [sb1(f"sing{i}", [128, 1024], F32) for i in range(2)]
            wq_sb = sb1("wq_sb", [128, EC * HPC * 128], BF16)
            wk_sb = sb1("wk_sb", [128, EC * HPC * 128], BF16)
            wv_sb = sb1("wv_sb", [128, EC * 1024], BF16)
            x_t = [sb1(f"x_t{i}", [128, E], F32) for i in range(2)]
            xln = [sb1(f"xln{i}", [128, E], BF16) for i in range(2)]
            xlnT = [sb1(f"xlnT{i}", [128, EC * 512], BF16) for i in range(2)]
            kst = [sb1(f"kst{i}", [128, 1024], F32) for i in range(2)]
            krot = [sb1(f"krot{i}", [128, 1024], F32) for i in range(2)]
            kbf = [sb1(f"kbf{i}", [128, 1024], BF16) for i in range(2)]
            vbf = [sb1(f"vbf{i}", [128, 1024], BF16) for i in range(2)]
            bqs = sb1("bqs", [128, HPC], F32)
            bks = sb1("bks", [128, HPC], F32)
            bvb = sb1("bvb", [128, 1024], F32)
            sqs = sb1("sqs", [128, 1024], F32)
            st1 = sb1("st1", [128, 8], F32)
            psP = [ps1(f"psP{i}", [128, 512]) for i in range(4)]
            psV = [ps1(f"psV{i}", [128, 512]) for i in range(2)]
            psTr = [c1.enter_context(nc.psum_tensor(f"psTr{i}", [128, 512], BF16))
                    for i in range(2)]

            S.op("sync", lambda sy: sy.dma_start(out=bqs[:], in_=bq_d), inc="s_w", inc_n=16)
            bk_w = S.op("sync", lambda sy: sy.dma_start(out=bks[:], in_=bk_d), inc="s_w", inc_n=16)
            S.op("sync", lambda sy: sy.dma_start(out=rowbuf[0:1, 0:1024], in_=bv_d),
                 inc="s_w", inc_n=16)
            for src_d, dst_sb in ((wq_d, wq_sb), (wk_d, wk_sb)):
                for eh in range(2):
                    S.op("gpsimd", lambda g, src_d=src_d, dst_sb=dst_sb, eh=eh: g.dma_start(
                        out=dst_sb[:, eh * 8192:(eh + 1) * 8192]
                        .rearrange("p (e f c) -> p e f c", e=EC // 2, f=HPC),
                        in_=src_d[:, eh * 8:(eh + 1) * 8]),
                        inc="s_wg", inc_n=16)
            S.op("gpsimd", lambda g: g.dma_start(out=wv_sb[:].rearrange("p (e c) -> p e c", e=EC), in_=wv_d),
                 inc="s_wg", inc_n=16)
            w_done = S.cnt["s_w"]
            wg_done = S.cnt["s_wg"]
            bks_ve = S.op("vector", lambda v: v.tensor_scalar_mul(bks[:], bks[:], RSCALE),
                          waits=[("s_w", bk_w)], inc="s_ve")
            bvb_pe = 0
            for nn in range(2):
                bvb_pe = S.op("tensor", lambda t, nn=nn: t.matmul(
                    psV[nn][:], ones128[0:1, :],
                    rowbuf[0:1, nn * 512:(nn + 1) * 512], start=True, stop=True),
                    waits=[("s_w", w_done), ("s_ve", init_ve)], inc="s_pe")
            bvb_ve = 0
            for nn in range(2):
                bvb_ve = S.op("vector", lambda v, nn=nn: v.tensor_copy(
                    bvb[:, nn * 512:(nn + 1) * 512], psV[nn][:]),
                    waits=[("s_pe", bvb_pe)], inc="s_ve")

            ln_done = [0] * TCH
            trp_done = [0] * TCH
            trev_done = [0] * TCH
            pr_done = [0] * TGR
            evac_free = {}
            tr_bank_free = {}
            kq_c3 = {}
            kq_c5 = {}
            kbf_free = {}
            vbf_free = [0, 0]
            vps_free = {0: bvb_ve, 1: bvb_ve}
            kq_uses = 0
            rstd_prev = 0
            r4_prev = 0
            rope_read = [(0, 0), (0, 0)]
            cs_ld = [0] * TGR
            for tc in range(TCH):
                g = tc // 4
                if tc % 4 == 0:
                    rve, rgp = rope_read[g % 2]
                    waits = []
                    if rve:
                        waits += [("s_ve", rve), ("s_gp", rgp)]
                    for nn in range(2):
                        S.op("sync", lambda sy, g=g, nn=nn: sy.dma_start(
                            out=cosg[g % 2][:, nn * 512:(nn + 1) * 512],
                            in_=cos_d[:, g * 512:(g + 1) * 512]),
                            waits=waits if nn == 0 else (), inc="s_x", inc_n=16)
                        cs_ld[g] = S.op("sync", lambda sy, g=g, nn=nn: sy.dma_start(
                            out=sing[g % 2][:, nn * 512:(nn + 1) * 512],
                            in_=sin_d[:, g * 512:(g + 1) * 512]),
                            inc="s_x", inc_n=16)
                waits = [("s_ve", ln_done[tc - 2])] if tc >= 2 else []
                x_ld = S.op("sync", lambda sy, tc=tc: sy.dma_start(
                    out=x_t[tc % 2][:], in_=xfull[tc * 128:(tc + 1) * 128, :]),
                    waits=waits, inc="s_x", inc_n=16)
                waits = [("s_x", x_ld)]
                if r4_prev:
                    waits.append(("s_ve", r4_prev))
                S.op("scalar", lambda s, tc=tc: s.activation(
                    sqs[:], x_t[tc % 2][:, 0:1024], AF.Square, accum_out=st1[:, 4:5]),
                    waits=waits, inc="s_ac")
                sq2 = S.op("scalar", lambda s, tc=tc: s.activation(
                    sqs[:], x_t[tc % 2][:, 1024:2048], AF.Square, accum_out=st1[:, 5:6]),
                    inc="s_ac")
                r1 = S.op("vector", lambda v, tc=tc: v.reduce_sum(
                    st1[:, 0:1], x_t[tc % 2][:], AX),
                    waits=[("s_x", x_ld)], inc="s_ve")
                r2 = S.op("vector", lambda v: v.tensor_scalar_mul(st1[:, 0:1], st1[:, 0:1], 1.0 / E),
                          waits=[("s_ve", r1)], inc="s_ve")
                S.op("vector", lambda v: v.tensor_mul(st1[:, 1:2], st1[:, 0:1], st1[:, 0:1]),
                     waits=[("s_ve", r2)], inc="s_ve")
                waits = [("s_ac", sq2)]
                if rstd_prev:
                    waits.append(("s_ac", rstd_prev))
                r4_prev = S.op("vector", lambda v: v.tensor_add(
                    st1[:, 2:3], st1[:, 4:5], st1[:, 5:6]),
                    waits=waits, inc="s_ve")
                r4b = S.op("vector", lambda v: v.tensor_scalar_mul(
                    st1[:, 2:3], st1[:, 2:3], 1.0 / E),
                    waits=[("s_ve", r4_prev)], inc="s_ve")
                r5 = S.op("vector", lambda v: v.tensor_sub(
                    st1[:, 2:3], st1[:, 2:3], st1[:, 1:2]),
                    waits=[("s_ve", r4b)], inc="s_ve")
                a1 = S.op("scalar", lambda s: s.activation(
                    st1[:, 3:4], st1[:, 2:3], AF.Ln, bias=eps_t[:, 0:1]),
                    waits=[("s_ve", r5)], inc="s_ac")
                rstd_prev = S.op("scalar", lambda s: s.activation(
                    st1[:, 3:4], st1[:, 3:4], AF.Exp, scale=-0.5),
                    waits=[("s_ac", a1)], inc="s_ac")
                waits = [("s_ac", rstd_prev)]
                if tc >= 2:
                    waits.append(("s_pe", trp_done[tc - 2]))
                ln_done[tc] = S.op("vector", lambda v, tc=tc: v.tensor_scalar(
                    xln[tc % 2][:], x_t[tc % 2][:], st1[:, 0:1], st1[:, 3:4],
                    ALU.subtract, ALU.mult),
                    waits=waits, inc="s_ve")
                for e in range(EC):
                    b = (e // 4) % 2
                    waits = []
                    if e == 0:
                        waits.append(("s_ve", ln_done[tc]))
                    if e % 4 == 0 and tr_bank_free.get(b):
                        waits.append(("s_ac", tr_bank_free[b]))
                    trp_done[tc] = S.op("tensor", lambda t, tc=tc, e=e, b=b: t.transpose(
                        psTr[b][:, (e % 4) * 128:(e % 4 + 1) * 128],
                        xln[tc % 2][:, e * 128:(e + 1) * 128], identbf[:]),
                        waits=waits, inc="s_pe")
                    if e % 4 == 3:
                        waits = [("s_pe", trp_done[tc])]
                        if e == 3 and g >= 2:
                            waits.append(("s_pe", pr_done[g - 2]))
                        tr_bank_free[b] = S.op("scalar", lambda s, tc=tc, e=e, b=b, g=g:
                                               s.copy(xlnT[g % 2]
                                                      .rearrange("p (e c) -> p e c", e=EC)
                                                      [:, e - 3:e + 1, (tc % 4) * 128:(tc % 4) * 128 + 128],
                                                      psTr[b][:].rearrange("p (e c) -> p e c", e=4)),
                                               waits=waits, inc="s_ac")
                trev_done[tc] = S.cnt["s_ac"]

                if tc % 4 != 3:
                    continue

                xg = xlnT[g % 2]
                base_waits = [("s_ac", trev_done[tc])]
                if g == 0:
                    base_waits += [("s_w", w_done), ("s_wg", wg_done),
                                   ("s_ve", max(bvb_ve, bks_ve))]

                for h in range(HPC):
                    bankK = (2 * h) % 4
                    bankQ = (2 * h + 1) % 4
                    stg = h % 2
                    waits = list(base_waits if h == 0 else [])
                    if evac_free.get(bankK):
                        waits.append(("s_ac", evac_free[bankK]))

                    def mmk(t, h=h, bankK=bankK, xg=xg):
                        for e in range(EC):
                            mm = t.matmul(psP[bankK][:],
                                          wk_sb[:, (e * HPC + h) * 128:(e * HPC + h) * 128 + 128],
                                          xg[:, e * 512:(e + 1) * 512],
                                          start=(e == 0), stop=(e == EC - 1))
                        return mm
                    mmk_pe = S.op("tensor", mmk, waits=waits, inc="s_pe")
                    waits = []
                    if evac_free.get(bankQ):
                        waits.append(("s_ac", evac_free[bankQ]))

                    def mmq(t, h=h, bankQ=bankQ, xg=xg):
                        for e in range(EC):
                            mm = t.matmul(psP[bankQ][:],
                                          wq_sb[:, (e * HPC + h) * 128:(e * HPC + h) * 128 + 128],
                                          xg[:, e * 512:(e + 1) * 512],
                                          start=(e == 0), stop=(e == EC - 1))
                        return mm
                    mmq_pe = S.op("tensor", mmq, waits=waits, inc="s_pe")

                    waits = [("s_pe", mmk_pe)]
                    if kq_c3.get(stg):
                        waits.append(("s_ve", kq_c3[stg]))
                    evK = S.op("scalar", lambda s, bankK=bankK, stg=stg, h=h:
                               s.activation(kst[stg][:, 0:512], psP[bankK][:], AF.Identity,
                                            bias=bks[:, h:h + 1], scale=RSCALE),
                               waits=waits, inc="s_ac")
                    evac_free[bankK] = evK
                    evQ = S.op("scalar", lambda s, bankQ=bankQ, stg=stg, h=h:
                               s.activation(kst[stg][:, 512:1024], psP[bankQ][:], AF.Identity,
                                            bias=bqs[:, h:h + 1], scale=1.0),
                               waits=[("s_pe", mmq_pe)], inc="s_ac")
                    evac_free[bankQ] = evQ
                    waits = [("s_ac", evQ)]
                    if kq_c5.get(stg):
                        waits.append(("s_ve", kq_c5[stg]))
                    S.op("scalar", lambda s, stg=stg: s.copy(
                        krot[stg][0:64, :], kst[stg][64:128, :]), waits=waits, inc="s_ac")
                    c2 = S.op("scalar", lambda s, stg=stg: s.copy(
                        krot[stg][64:128, :], kst[stg][0:64, :]), inc="s_ac")
                    c3 = S.op("vector", lambda v, stg=stg, g=g: v.tensor_mul(
                        kst[stg][:], kst[stg][:], cosg[g % 2][:]),
                        waits=[("s_ac", c2), ("s_x", cs_ld[g])], inc="s_ve")
                    c4 = S.op("gpsimd", lambda gq, stg=stg, g=g: gq.tensor_mul(
                        krot[stg][:], krot[stg][:], sing[g % 2][:]),
                        waits=[("s_ac", c2), ("s_x", cs_ld[g])], inc="s_gp")
                    waits = [("s_ve", c3), ("s_gp", c4)]
                    if kbf_free.get(stg):
                        waits.append(("s_kv", kbf_free[stg]))
                    c5 = S.op("vector", lambda v, stg=stg: v.tensor_add(
                        kbf[stg][:], kst[stg][:], krot[stg][:]),
                        waits=waits, inc="s_ve")
                    kq_c3[stg] = c5
                    kq_c5[stg] = c5
                    S.op("gpsimd", lambda gq, h=h, g=g, stg=stg:
                         gq.dma_start(out=kscr[h, :, g * 512:(g + 1) * 512],
                                      in_=kbf[stg][:, 0:512]),
                         waits=[("s_ve", c5)], inc="s_kv", inc_n=16)
                    kbf_free[stg] = S.op("gpsimd", lambda gq, h=h, g=g, stg=stg:
                                         gq.dma_start(out=qscr[h, :, g * 512:(g + 1) * 512],
                                                      in_=kbf[stg][:, 512:1024]),
                                         inc="s_kv", inc_n=16)
                    kq_uses += 1
                rope_read[g % 2] = (c3, c4)

                for tsub in range(4):
                    tci = g * 4 + tsub
                    vmm = [0, 0]
                    for nn in range(2):
                        waits = list(base_waits if (tsub == 0 and nn == 0) else [])
                        if vps_free[nn]:
                            waits.append(("s_ve", vps_free[nn]))

                        def mmv(t, nn=nn, tsub=tsub, xg=xg):
                            for e in range(EC):
                                mm = t.matmul(psV[nn][:],
                                              xg[:, e * 512 + tsub * 128:e * 512 + tsub * 128 + 128],
                                              wv_sb[:, e * 1024 + nn * 512:e * 1024 + (nn + 1) * 512],
                                              start=(e == 0), stop=(e == EC - 1))
                            return mm
                        vmm[nn] = S.op("tensor", mmv, waits=waits, inc="s_pe")
                    vev = 0
                    for nn in range(2):
                        waits = [("s_pe", vmm[nn])]
                        if nn == 0 and vbf_free[tci % 2]:
                            waits.append(("s_kv", vbf_free[tci % 2]))
                        vev = S.op("vector", lambda v, nn=nn, tci=tci: v.tensor_add(
                            vbf[tci % 2][:, nn * 512:(nn + 1) * 512], psV[nn][:],
                            bvb[:, nn * 512:(nn + 1) * 512]), waits=waits, inc="s_ve")
                        vps_free[nn] = vev
                    vbf_free[tci % 2] = S.op("gpsimd", lambda gq, tci=tci: gq.dma_start(
                        out=vscr[tci], in_=vbf[tci % 2][:]),
                        waits=[("s_ve", vev)], inc="s_kv", inc_n=16)
                pr_done[g] = S.cnt["s_pe"]

            p1_kv = S.cnt["s_kv"]
            p1_tr = S.cnt["s_tr"]
            S.replay(block)

        with contextlib.ExitStack() as cB:

            with contextlib.ExitStack() as c2:
                sb2 = lambda n, sh, d_: c2.enter_context(nc.sbuf_tensor(n, sh, d_))
                ps2 = lambda n, sh: c2.enter_context(nc.psum_tensor(n, sh, F32))
                kT2 = [sb2(f"kT2_{i}", [128, T], BF16) for i in range(2)]
                qT2 = [sb2(f"qT2_{i}", [128, T], BF16) for i in range(2)]
                vt2 = [sb2(f"vt2_{i}", [128, T], BF16) for i in range(2)]
                y_sb = sb2("y_sb", [128, HPC * T], BF16)
                pT_bf = [sb2(f"pT_bf{i}", [128, 512], BF16) for i in range(4)]
                rlb = [sb2(f"rlb{i}", [128, 512], F32) for i in range(2)]
                psT = [ps2(f"psT{i}", [128, 512]) for i in range(4)]
                psO = [ps2(f"psO{i}", [128, 512]) for i in range(2)]
                psL = [ps2(f"psL{i}", [128, 512]) for i in range(2)]

                head_loads = [0] * HPC
                head_pe_done = [0] * HPC
                exp_free = {}
                pv_free = {}
                grp_epi = {0: 0, 1: 0}
                gidx = 0
                y_dma = [0] * HPC

                def emit_head_loads(h):
                    waits = [("s_kv", p1_kv)]
                    if h >= 2:
                        waits.append(("s_pe", head_pe_done[h - 2]))
                    S.op("sync", lambda sy, h=h: sy.dma_start(out=kT2[h % 2][:], in_=kscr[h]),
                         waits=waits, inc="s_kvr", inc_n=16)
                    S.op("sync", lambda sy, h=h: sy.dma_start(out=qT2[h % 2][:], in_=qscr[h]),
                         inc="s_kvr", inc_n=16)
                    head_loads[h] = S.op("sync", lambda sy, h=h: sy.dma_start(
                        out=vt2[h % 2][:].rearrange("p (a c) -> p a c", a=TCH),
                        in_=vscr[:, :, h * 128:(h + 1) * 128].rearrange("a p c -> p a c")),
                        inc="s_kvr", inc_n=16)

                emit_head_loads(0)
                for h in range(HPC):
                    if h + 1 < HPC:
                        emit_head_loads(h + 1)
                    kk, qq, vv = kT2[h % 2], qT2[h % 2], vt2[h % 2]

                    for g2 in range(TGR):
                        nkt = 4 * g2 + 4
                        pb = gidx % 2
                        gidx += 1
                        exps = [0] * nkt
                        lastL = [0]
                        lastV = [0]

                        def emit_LV(kt, g2=g2, nkt=nkt, pb=pb, vv=vv,
                                    exps=exps, lastL=lastL, lastV=lastV):
                            qoff = max(0, (kt - 4 * g2) * 128)
                            buf = kt % 4
                            waits = [("s_ac", exps[kt])]
                            if kt == 0 and grp_epi[pb]:
                                waits.append(("s_ve", grp_epi[pb]))
                            lastL[0] = S.op("tensor", lambda t, kt=kt, qoff=qoff, buf=buf:
                                            t.matmul(psL[pb][:, qoff:512], onesbf[:],
                                                     pT_bf[buf][:, qoff:512],
                                                     start=(kt == 0), stop=(kt == nkt - 1)),
                                            waits=waits, inc="s_pe")
                            lastV[0] = S.op("tensor", lambda t, kt=kt, qoff=qoff, buf=buf:
                                            t.matmul(psO[pb][:, qoff:512],
                                                     vv[:, kt * 128:(kt + 1) * 128],
                                                     pT_bf[buf][:, qoff:512],
                                                     start=(kt == 0), stop=(kt == nkt - 1)),
                                            inc="s_pe")
                            pv_free[buf] = lastV[0]

                        for kt in range(nkt):
                            qoff = max(0, (kt - 4 * g2) * 128)
                            bank = kt % 4
                            waits = [("s_kvr", head_loads[h])]
                            if exp_free.get(bank):
                                waits.append(("s_ac", exp_free[bank]))
                            mmS = S.op("tensor", lambda t, kt=kt, qoff=qoff, bank=bank,
                                       g2=g2, kk=kk, qq=qq: t.matmul(
                                           psT[bank][:, qoff:512],
                                           kk[:, kt * 128:(kt + 1) * 128],
                                           qq[:, g2 * 512 + qoff:(g2 + 1) * 512],
                                           start=True, stop=True),
                                       waits=waits, inc="s_pe")
                            dep = [("s_pe", mmS)]
                            if kt >= 4 * g2:
                                mk = S.op("vector", lambda v, bank=bank, qoff=qoff: v.tensor_add(
                                    psT[bank][:, qoff:qoff + 128],
                                    psT[bank][:, qoff:qoff + 128], maskT[:]),
                                    waits=dep, inc="s_ve")
                                dep = [("s_ve", mk)]
                            waits = list(dep)
                            if pv_free.get(bank):
                                waits.append(("s_pe", pv_free[bank]))
                            exps[kt] = S.op("scalar", lambda s, bank=bank, qoff=qoff:
                                            s.activation(pT_bf[bank][:, qoff:512],
                                                         psT[bank][:, qoff:512], AF.Exp),
                                            waits=waits, inc="s_ac")
                            exp_free[bank] = exps[kt]
                            if kt >= 2:
                                emit_LV(kt - 2)
                        emit_LV(nkt - 2)
                        emit_LV(nkt - 1)

                        waits = [("s_pe", lastL[0])]
                        if grp_epi[pb]:
                            waits.append(("s_ve", grp_epi[pb]))
                        rc = S.op("vector", lambda v, pb=pb: v.reciprocal(
                            rlb[pb][:], psL[pb][:]), waits=waits, inc="s_ve")
                        grp_epi[pb] = S.op("vector", lambda v, h=h, g2=g2, pb=pb: v.tensor_mul(
                            y_sb[:, h * T + g2 * 512:h * T + (g2 + 1) * 512],
                            psO[pb][:], rlb[pb][:]),
                            waits=[("s_pe", lastV[0]), ("s_ve", rc)], inc="s_ve")

                    head_pe_done[h] = S.cnt["s_pe"]
                    agt = ag_i1 if h < 4 else ag_i2
                    hr = h % 4
                    y_dma[h] = S.op("sync", lambda sy, h=h, agt=agt, hr=hr: sy.dma_start(
                        out=agt[hr * 128:(hr + 1) * 128, :], in_=y_sb[:, h * T:(h + 1) * T]),
                        waits=[("s_ve", grp_epi[(gidx - 1) % 2])], inc="s_y", inc_n=16)
                    if h == 3:
                        S.op("gpsimd", lambda g: g.collective_compute(
                            "AllGather", ALU.bypass,
                            replica_groups=[[0, 1], [2, 3], [4, 5], [6, 7]],
                            ins=[ag_i1.opt()], outs=[ag_o1.opt()]),
                            waits=[("s_y", y_dma[3])], inc="s_cc")

                cc_done = S.op("gpsimd", lambda g: g.collective_compute(
                    "AllGather", ALU.bypass,
                    replica_groups=[[0, 1], [2, 3], [4, 5], [6, 7]],
                    ins=[ag_i2.opt()], outs=[ag_o2.opt()]),
                    waits=[("s_y", y_dma[HPC - 1])], inc="s_cc")
                if dump_scr:
                    for src, dst in [(kscr, dmp_k), (qscr, dmp_q)]:
                        S.op("sync", lambda sy, s=src, d=dst: sy.dma_start(out=d, in_=s),
                             waits=[("s_kv", p1_kv)], inc="s_dmp", inc_n=16)
                    S.op("sync", lambda sy: sy.dma_start(out=dmp_v, in_=vscr),
                         waits=[("s_kv", p1_kv)], inc="s_dmp", inc_n=16)
                p2_ve = S.cnt["s_ve"]
                p2_ac = S.cnt["s_ac"]
                S.replay(block)

            with contextlib.ExitStack() as cC:
                sbC = lambda n, sh, d_: cC.enter_context(nc.sbuf_tensor(n, sh, d_))
                xln2T = sbC("xln2T", [128, EC * MYT], BF16)
                w_all = sbC("w_all", [128, MTC * NEXP], F32)
                gate_sb = sbC("gate_sb", [128, EC * NEXP], BF16)

                with contextlib.ExitStack() as c3:
                    sb3 = lambda n, sh, d_: c3.enter_context(nc.sbuf_tensor(n, sh, d_))
                    ps3 = lambda n, sh: c3.enter_context(nc.psum_tensor(n, sh, F32))
                    woT_sb = sb3("woT_sb", [128, EC * E], BF16)
                    bob = sb3("bob", [128, E], F32)
                    xm = sb3("xm", [128, E], F32)
                    y2ch = [sb3(f"y2ch{i}", [128, EC * 256], BF16) for i in range(2)]
                    y2bh = sb3("y2bh", [128, EC * 256], BF16)
                    x1 = [sb3(f"x1_{i}", [128, E], F32) for i in range(2)]
                    xln2 = [sb3(f"xln2_{i}", [128, E], BF16) for i in range(2)]
                    sqs3 = sb3("sqs3", [128, 1024], F32)
                    st3 = sb3("st3", [128, 8], F32)
                    sc4 = sb3("sc4", [128, 4 * NEXP], F32)
                    t4 = sb3("t4", [128, 4], F32)
                    psW = [ps3(f"psW{i}", [128, 512]) for i in range(2)]
                    psG = ps3("psG", [128, 512])
                    psTr3 = [c3.enter_context(nc.psum_tensor(f"psTr3_{i}", [128, 512], BF16))
                             for i in range(2)]

                    for _eng in ("tensor", "vector", "scalar"):
                        S.op(_eng, lambda e: None, waits=[("s_cc", cc_done)])
                    wo_w = S.op("gpsimd", lambda g: g.dma_start(
                        out=woT_sb[:].rearrange("p (e c) -> p e c", e=EC),
                        in_=wo_d.rearrange("e p c -> p e c")),
                        waits=[("s_y", S.cnt["s_y"]), ("s_ve", p2_ve), ("s_ac", p2_ac)],
                        inc="s_wg", inc_n=16)
                    gate_w = S.op("gpsimd", lambda g: g.dma_start(
                        out=gate_sb[:].rearrange("p (e c) -> p e c", e=EC), in_=gate_d),
                        waits=[("s_cc", cc_done)], inc="s_wg", inc_n=16)
                    bob_ve = 0
                    for jj in range(2):
                        bo_w = S.op("sync", lambda sy, jj=jj: sy.dma_start(
                            out=rowbuf[:], in_=bo_d[0:1, jj * 1024:(jj + 1) * 1024]),
                            waits=[("s_cc", cc_done)] + ([("s_pe", bmm)] if jj else []),
                            inc="s_w", inc_n=16)
                        for j2 in range(2):
                            j = jj * 2 + j2
                            bmm = S.op("tensor", lambda t, j2=j2: t.matmul(
                                psW[0][:, 0:512], ones128[0:1, :],
                                rowbuf[0:1, j2 * 512:(j2 + 1) * 512], start=True, stop=True),
                                waits=[("s_w", bo_w)] + ([("s_ve", bob_ve)] if bob_ve else []),
                                inc="s_pe")
                            bob_ve = S.op("vector", lambda v, j=j: v.tensor_copy(
                                bob[:, j * 512:(j + 1) * 512], psW[0][:]),
                                waits=[("s_pe", bmm)], inc="s_ve")

                    x1_free = [0, 0]
                    x1_out = [0, 0]
                    wo_rd = [0, 0]
                    sel_rd = 0
                    xln2_free = [0, 0]
                    psW_free = {0: bob_ve, 1: 0}
                    tr3_bank_free = {}
                    xm_free = 0
                    tr3_ev = [0] * MTC
                    trp3 = [0] * MTC
                    psG_free = 0
                    out_init = 0
                    rstd3_prev = 0
                    r43_prev = 0

                    for ch in range(MTC // 2):
                        lda = ldb = 0
                        for fb in range(4):
                            src, r0 = [(ag_o1, 0), (ag_o2, 0),
                                       (ag_o1, 512), (ag_o2, 512)][fb]
                            waits = [("s_cc", 2 if ch == 0 else (1 if fb % 2 == 0 else 2))]
                            if fb == 0 and wo_rd[ch % 2]:
                                waits.append(("s_pe", wo_rd[ch % 2]))
                            lda = S.op("sync", lambda sy, ch=ch, fb=fb, src=src, r0=r0:
                                       sy.dma_start(
                                           out=y2ch[ch % 2][:, fb * 1024:(fb + 1) * 1024]
                                           .rearrange("p (f c) -> p f c", f=4),
                                           in_=src[r0:r0 + 512, ch * 256:(ch + 1) * 256]
                                           .rearrange("(f p) c -> p f c", p=128)),
                                       waits=waits, inc="s_kvr", inc_n=16)
                            waits = [("s_cc", 2 if ch == 0 else (1 if fb % 2 == 0 else 2))]
                            if fb == 0 and sel_rd:
                                waits.append(("s_ve", sel_rd))
                            ldb = S.op("scalar", lambda s, ch=ch, fb=fb, src=src, r0=r0:
                                       s.dma_start(
                                           out=y2bh[:, fb * 1024:(fb + 1) * 1024]
                                           .rearrange("p (f c) -> p f c", f=4),
                                           in_=src[r0:r0 + 512, MYT + ch * 256:MYT + (ch + 1) * 256]
                                           .rearrange("(f p) c -> p f c", p=128)),
                                       waits=waits, inc="s_tr", inc_n=16)
                        s1v = S.op("vector", lambda v, ch=ch: v.tensor_scalar(
                            y2ch[ch % 2][:], y2ch[ch % 2][:], par0c[:, 0:1], None, ALU.mult),
                            waits=[("s_kvr", lda)], inc="s_ve")
                        sel_done = S.op("vector", lambda v, ch=ch: v.scalar_tensor_tensor(
                            y2ch[ch % 2][:], y2bh[:], par1c[:, 0:1], y2ch[ch % 2][:],
                            ALU.mult, ALU.add),
                            waits=[("s_ve", s1v), ("s_tr", ldb)], inc="s_ve")
                        sel_rd = sel_done

                        for tci in range(2):
                            tc = 2 * ch + tci
                            waits = [("s_cc", cc_done)] if tc == 0 else []
                            if xm_free:
                                waits.append(("s_ve", xm_free))
                            xml = S.op("sync", lambda sy, tc=tc: sy.dma_start(
                                out=xm[:], in_=xmine[tc * 128:(tc + 1) * 128, :]),
                                waits=waits, inc="s_xm", inc_n=16)

                            wo_mm = 0
                            for oc in range(4):
                                waits = [("s_ve", sel_done)]
                                if oc == 0 and tc == 0:
                                    waits.append(("s_wg", wo_w))
                                if psW_free[oc % 2]:
                                    waits.append(("s_ve", psW_free[oc % 2]))

                                def mmw(t, ch=ch, tci=tci, oc=oc):
                                    for f in range(EC):
                                        mm = t.matmul(psW[oc % 2][:],
                                                      y2ch[ch % 2][:, f * 256 + tci * 128:f * 256 + tci * 128 + 128],
                                                      woT_sb[:, f * E + oc * 512:f * E + (oc + 1) * 512],
                                                      start=(f == 0), stop=(f == EC - 1))
                                    return mm
                                wo_mm = S.op("tensor", mmw, waits=waits, inc="s_pe")
                                waits = [("s_pe", wo_mm)]
                                if oc == 0:
                                    if x1_free[tc % 2]:
                                        waits.append(("s_ve", x1_free[tc % 2]))
                                    if x1_out[tc % 2]:
                                        waits.append(("s_out", x1_out[tc % 2]))
                                ev = S.op("vector", lambda v, tc=tc, oc=oc: v.scalar_tensor_tensor(
                                    x1[tc % 2][:, oc * 512:(oc + 1) * 512], psW[oc % 2][:], 1.0,
                                    bob[:, oc * 512:(oc + 1) * 512], ALU.mult, ALU.add),
                                    waits=waits, inc="s_ve")
                                psW_free[oc % 2] = ev
                            wo_rd[ch % 2] = wo_mm
                            xm_free = S.op("vector", lambda v, tc=tc: v.tensor_add(
                                x1[tc % 2][:], x1[tc % 2][:], xm[:]),
                                waits=[("s_ve", psW_free[1]), ("s_ve", psW_free[0]),
                                       ("s_xm", xml)], inc="s_ve")
                            x1_done = xm_free
                            x1_out[tc % 2] = S.op("gpsimd", lambda g, tc=tc: g.dma_start(
                                out=out[tc * 128:(tc + 1) * 128, :], in_=x1[tc % 2][:]),
                                waits=[("s_ve", x1_done)], inc="s_out", inc_n=16)

                            waits = [("s_ve", x1_done)]
                            if r43_prev:
                                waits.append(("s_ve", r43_prev))
                            S.op("scalar", lambda s, tc=tc: s.activation(
                                sqs3[:], x1[tc % 2][:, 0:1024], AF.Square,
                                accum_out=st3[:, 4:5]),
                                waits=waits, inc="s_ac")
                            sq2 = S.op("scalar", lambda s, tc=tc: s.activation(
                                sqs3[:], x1[tc % 2][:, 1024:2048], AF.Square,
                                accum_out=st3[:, 5:6]),
                                inc="s_ac")
                            r1 = S.op("vector", lambda v, tc=tc: v.reduce_sum(
                                st3[:, 0:1], x1[tc % 2][:], AX),
                                waits=[("s_ve", x1_done)], inc="s_ve")
                            r2 = S.op("vector", lambda v: v.tensor_scalar_mul(
                                st3[:, 0:1], st3[:, 0:1], 1.0 / E),
                                waits=[("s_ve", r1)], inc="s_ve")
                            S.op("vector", lambda v: v.tensor_mul(
                                st3[:, 1:2], st3[:, 0:1], st3[:, 0:1]),
                                waits=[("s_ve", r2)], inc="s_ve")
                            waits = [("s_ac", sq2)]
                            if rstd3_prev:
                                waits.append(("s_ac", rstd3_prev))
                            r43_prev = S.op("vector", lambda v: v.tensor_add(
                                st3[:, 2:3], st3[:, 4:5], st3[:, 5:6]),
                                waits=waits, inc="s_ve")
                            r4b = S.op("vector", lambda v: v.tensor_scalar_mul(
                                st3[:, 2:3], st3[:, 2:3], 1.0 / E),
                                waits=[("s_ve", r43_prev)], inc="s_ve")
                            r5 = S.op("vector", lambda v: v.tensor_sub(
                                st3[:, 2:3], st3[:, 2:3], st3[:, 1:2]),
                                waits=[("s_ve", r4b)], inc="s_ve")
                            a1 = S.op("scalar", lambda s: s.activation(
                                st3[:, 3:4], st3[:, 2:3], AF.Ln, bias=eps_t[:, 0:1]),
                                waits=[("s_ve", r5)], inc="s_ac")
                            rstd3_prev = S.op("scalar", lambda s: s.activation(
                                st3[:, 3:4], st3[:, 3:4], AF.Exp, scale=-0.5),
                                waits=[("s_ac", a1)], inc="s_ac")
                            waits = [("s_ac", rstd3_prev)]
                            if xln2_free[tc % 2]:
                                waits.append(("s_pe", xln2_free[tc % 2]))
                            ln2_ap = S.op("vector", lambda v, tc=tc: v.tensor_scalar(
                                xln2[tc % 2][:], x1[tc % 2][:], st3[:, 0:1], st3[:, 3:4],
                                ALU.subtract, ALU.mult),
                                waits=waits, inc="s_ve")
                            x1_free[tc % 2] = ln2_ap
                            for e in range(EC):
                                b = (e // 4) % 2
                                waits = []
                                if e == 0:
                                    waits.append(("s_ve", ln2_ap))
                                if e % 4 == 0 and tr3_bank_free.get(b):
                                    waits.append(("s_ac", tr3_bank_free[b]))
                                trp3[tc] = S.op("tensor", lambda t, tc=tc, e=e, b=b: t.transpose(
                                    psTr3[b][:, (e % 4) * 128:(e % 4 + 1) * 128],
                                    xln2[tc % 2][:, e * 128:(e + 1) * 128], identbf[:]),
                                    waits=waits, inc="s_pe")
                                if e % 4 == 3:
                                    tr3_bank_free[b] = S.op(
                                        "scalar", lambda s, tc=tc, e=e, b=b: s.copy(
                                            xln2T.rearrange("p (e c) -> p e c", e=EC)
                                            [:, e - 3:e + 1, tc * 128:(tc + 1) * 128],
                                            psTr3[b][:].rearrange("p (e c) -> p e c", e=4)),
                                        waits=[("s_pe", trp3[tc])], inc="s_ac")
                            tr3_ev[tc] = S.cnt["s_ac"]
                            xln2_free[tc % 2] = trp3[tc]

                            waits = [("s_ac", tr3_ev[tc]), ("s_wg", gate_w)]
                            if psG_free:
                                waits.append(("s_ve", psG_free))

                            def mmg(t, tc=tc):
                                for e in range(EC):
                                    mm = t.matmul(psG[:, 0:NEXP],
                                                  xln2T[:, e * MYT + tc * 128:e * MYT + (tc + 1) * 128],
                                                  gate_sb[:, e * NEXP:(e + 1) * NEXP],
                                                  start=(e == 0), stop=(e == EC - 1))
                                return mm
                            gmm = S.op("tensor", mmg, waits=waits, inc="s_pe")
                            sc = sc4
                            g0 = S.op("vector", lambda v: v.tensor_copy(sc[:, 0:16], psG[:, 0:16]),
                                      waits=[("s_pe", gmm)], inc="s_ve")
                            psG_free = g0
                            g1 = S.op("vector", lambda v: v.tensor_reduce(
                                t4[:, 0:1], sc[:, 0:16], AX, ALU.max, negate=True),
                                waits=[("s_ve", g0)], inc="s_ve")
                            g2x = S.op("scalar", lambda s: s.activation(
                                sc[:, 16:32], sc[:, 0:16], AF.Exp, bias=t4[:, 0:1],
                                accum_out=t4[:, 1:2]),
                                waits=[("s_ve", g1)], inc="s_ac")
                            g3 = S.op("vector", lambda v: v.reciprocal(t4[:, 2:3], t4[:, 1:2]),
                                      waits=[("s_ac", g2x)], inc="s_ve")
                            g4 = S.op("vector", lambda v: v.tensor_scalar(
                                sc[:, 32:48], sc[:, 16:32], t4[:, 2:3], None, ALU.mult),
                                waits=[("s_ve", g3)], inc="s_ve")
                            g5 = S.op("vector", lambda v: v.tensor_copy(sc[:, 48:64], sc[:, 32:48]),
                                      waits=[("s_ve", g4)], inc="s_ve")
                            prev = g5
                            for it in range(4):
                                m1 = S.op("vector", lambda v: v.tensor_reduce(
                                    t4[:, 3:4], sc[:, 48:64], AX, ALU.max),
                                    waits=[("s_ve", prev)], inc="s_ve")
                                if it < 3:
                                    e1 = S.op("vector", lambda v: v.tensor_scalar(
                                        sc[:, 16:32], sc[:, 48:64], t4[:, 3:4], None, ALU.is_ge),
                                        waits=[("s_ve", m1)], inc="s_ve")
                                    prev = S.op("vector", lambda v: v.scalar_tensor_tensor(
                                        sc[:, 48:64], sc[:, 16:32], -1e30, sc[:, 48:64],
                                        ALU.mult, ALU.add),
                                        waits=[("s_ve", e1)], inc="s_ve")
                                else:
                                    prev = m1
                            ge4 = S.op("vector", lambda v: v.tensor_scalar(
                                sc[:, 16:32], sc[:, 32:48], t4[:, 3:4], None, ALU.is_ge),
                                waits=[("s_ve", prev)], inc="s_ve")
                            S.op("vector", lambda v, tc=tc: v.tensor_mul(
                                w_all[:, tc * NEXP:(tc + 1) * NEXP], sc[:, 32:48], sc[:, 16:32]),
                                waits=[("s_ve", ge4)], inc="s_ve")

                    out_init = S.cnt["s_out"]
                    p3_tr = S.cnt["s_tr"]
                    p4_ve = S.cnt["s_ve"]
                    p4_ac = S.cnt["s_ac"]
                    S.replay(block)

                with contextlib.ExitStack() as c5:
                    sb5 = lambda n, sh, d_: c5.enter_context(nc.sbuf_tensor(n, sh, d_))
                    ps5 = lambda n, sh: c5.enter_context(nc.psum_tensor(n, sh, F32))
                    dn_sb = sb5("dn_sb", [128, FCH * E], BF16)
                    h_sb = sb5("h_sb", [128, FCH * MYT], BF16)
                    gu_ring = [sb5(f"gur{i}", [128, 2 * EC * 128], BF16) for i in range(2)]
                    ystage = [sb5(f"yst{i}", [128, E], F32) for i in range(2)]
                    sg = [sb5(f"sg{i}", [128, 512], BF16) for i in range(2)]
                    psGU = [ps5(f"psGU{i}", [128, 512]) for i in range(4)]
                    psDN = [ps5(f"psDN{i}", [128, 512]) for i in range(4)]

                    for _eng in ("tensor", "vector", "scalar"):
                        S.op(_eng, lambda e: None,
                             waits=[("s_out", out_init), ("s_tr", p3_tr),
                                    ("s_ac", p4_ac), ("s_xm", S.cnt["s_xm"]),
                                    ("s_kvr", S.cnt["s_kvr"])])
                    dn_free = 0
                    ring_free = [0, 0]
                    gu_free = {}
                    dnps_free = {}
                    sg_free = [0, 0]
                    yst_free = [0, 0]
                    gu_uses = 0

                    p5_fence = [("s_out", out_init), ("s_ve", p4_ve),
                                ("s_tr", p3_tr), ("s_ac", p4_ac),
                                ("s_xm", S.cnt["s_xm"]), ("s_kvr", S.cnt["s_kvr"])]
                    pair_recs = {}

                    def gsrc_of(xp2):
                        return guw_d[xp2] if xp2 < NEXP else sgu_d[xp2 - NEXP]

                    def emit_pair(xp2, fi, extra=()):
                        slot = (xp2 * FCH + fi) % 2
                        waits = list(extra)
                        if ring_free[slot]:
                            waits.append(("s_pe", ring_free[slot]))
                        pair_recs[(xp2, fi)] = S.op(
                            "gpsimd", lambda g, xp2=xp2, fi=fi, slot=slot: g.dma_start(
                                out=gu_ring[slot][:].rearrange("p (a e c) -> p a e c", a=2, e=EC),
                                in_=gsrc_of(xp2)[fi]),
                            waits=waits, inc="s_fw", inc_n=16)

                    emit_pair(0, 0, p5_fence)
                    emit_pair(0, 1, p5_fence)
                    for xp in range(n_exp):
                        acc_base = S.cnt["s_acc"]
                        shared = xp >= NEXP
                        dsrc = dnw_d[xp] if not shared else sdn_d[xp - NEXP]
                        waits = [("s_out", out_init)]
                        if xp == 0:
                            waits = list(p5_fence)
                        if dn_free:
                            waits.append(("s_pe", dn_free))
                        dn_rec = S.op("gpsimd", lambda g, dsrc=dsrc: g.dma_start(
                            out=dn_sb[:].rearrange("p (f c) -> p f c", f=FCH), in_=dsrc.rearrange("f p c -> p f c")),
                            waits=waits, inc="s_dn", inc_n=16)

                        h_writes = 0
                        for fi in range(FCH):
                            slot = (xp * FCH + fi) % 2
                            pair_rec = pair_recs[(xp, fi)]
                            last_mm = 0
                            for tg in range(4):
                                gb = (tg % 2) * 2
                                ub = gb + 1
                                for half in range(2):
                                    bank = gb + half
                                    waits = []
                                    if tg == 0 and half == 0:
                                        waits.append(("s_fw", pair_rec))
                                        if fi == 0 and xp == 0:
                                            waits.append(("s_ac", p4_ac))
                                    if gu_free.get(bank):
                                        waits.append(gu_free[bank])

                                    def mmgu(t, slot=slot, half=half, tg=tg, bank=bank):
                                        for e in range(EC):
                                            mm = t.matmul(
                                                psGU[bank][:],
                                                gu_ring[slot][:, half * 2048 + e * 128:half * 2048 + (e + 1) * 128],
                                                xln2T[:, e * MYT + tg * 512:e * MYT + (tg + 1) * 512],
                                                start=(e == 0), stop=(e == EC - 1))
                                        return mm
                                    last_mm = S.op("tensor", mmgu, waits=waits, inc="s_pe")
                                    if half == 0:
                                        waits = [("s_pe", last_mm)]
                                        if sg_free[tg % 2]:
                                            waits.append(("s_ve", sg_free[tg % 2]))
                                        sil = S.op("scalar", lambda s, tg=tg, gb=gb: s.activation(
                                            sg[tg % 2][:], psGU[gb][:], AF.Silu),
                                            waits=waits, inc="s_ac")
                                        gu_free[gb] = ("s_ac", sil)
                                    else:
                                        hv = S.op("vector", lambda v, fi=fi, tg=tg, ub=ub: v.tensor_mul(
                                            h_sb[:, fi * MYT + tg * 512:fi * MYT + (tg + 1) * 512],
                                            sg[tg % 2][:], psGU[ub][:]),
                                            waits=[("s_ac", sil), ("s_pe", last_mm)], inc="s_ve")
                                        gu_free[ub] = ("s_ve", hv)
                                        sg_free[tg % 2] = hv
                                        h_writes = hv
                            ring_free[slot] = last_mm
                            if fi + 2 < FCH:
                                emit_pair(xp, fi + 2)

                        if xp + 1 < n_exp:
                            emit_pair(xp + 1, 0)
                            emit_pair(xp + 1, 1)

                        dn_last = 0
                        for tc2 in range(MTC):
                            for oc in range(4):
                                bank = oc
                                waits = []
                                if tc2 == 0 and oc == 0:
                                    waits += [("s_ve", h_writes), ("s_dn", dn_rec)]
                                if dnps_free.get(bank):
                                    waits.append(("s_ve", dnps_free[bank]))

                                def mmdn(t, tc2=tc2, oc=oc, bank=bank):
                                    for fi in range(FCH):
                                        mm = t.matmul(
                                            psDN[bank][:],
                                            h_sb[:, fi * MYT + tc2 * 128:fi * MYT + (tc2 + 1) * 128],
                                            dn_sb[:, fi * E + oc * 512:fi * E + (oc + 1) * 512],
                                            start=(fi == 0), stop=(fi == FCH - 1))
                                    return mm
                                dmm = S.op("tensor", mmdn, waits=waits, inc="s_pe")
                                dn_last = dmm
                                waits = [("s_pe", dmm)]
                                if oc == 0 and yst_free[tc2 % 2]:
                                    waits.append(("s_acc", yst_free[tc2 % 2]))
                                if not shared:
                                    ystv = S.op("vector", lambda v, tc2=tc2, oc=oc, bank=bank, xp=xp:
                                                v.tensor_scalar(
                                                    ystage[tc2 % 2][:, oc * 512:(oc + 1) * 512],
                                                    psDN[bank][:],
                                                    w_all[:, tc2 * NEXP + xp:tc2 * NEXP + xp + 1],
                                                    None, ALU.mult),
                                                waits=[*waits], inc="s_ve")
                                else:
                                    ystv = S.op("vector", lambda v, tc2=tc2, oc=oc, bank=bank:
                                                v.tensor_copy(ystage[tc2 % 2][:, oc * 512:(oc + 1) * 512],
                                                              psDN[bank][:]),
                                                waits=[*waits], inc="s_ve")
                                dnps_free[bank] = ystv
                            yst_free[tc2 % 2] = S.op("gpsimd", lambda g, tc2=tc2: g.dma_start(
                                out=out[tc2 * 128:(tc2 + 1) * 128, :], in_=ystage[tc2 % 2][:],
                                accum_op=ALU.add),
                                waits=[("s_ve", S.cnt["s_ve"]),
                                       ("s_out", out_init), ("s_acc", acc_base)],
                                inc="s_acc", inc_n=16)
                        dn_free = dn_last

                    S.op("gpsimd", lambda g: None,
                         waits=[("s_acc", S.cnt["s_acc"]), ("s_out", S.cnt["s_out"]),
                                ("s_dmp", S.cnt["s_dmp"])])
                    S.replay(block)

    return nc


def _host_prep(L0_attn_x, L0_attn_Wq, L0_attn_bq, L0_attn_Wk, L0_attn_bk,
               L0_attn_Wv, L0_attn_bv, L0_attn_Wo, L0_attn_bo,
               L0_ffn_gate_weight, L0_ffn_gate_up_weights, L0_ffn_down_weights,
               L0_ffn_shared_gate_up, L0_ffn_shared_down):
    C = np.ascontiguousarray
    x = np.asarray(L0_attn_x, dtype=np.float32)
    Wq = np.asarray(L0_attn_Wq, dtype=np.float32)
    Wk = np.asarray(L0_attn_Wk, dtype=np.float32)
    Wv = np.asarray(L0_attn_Wv, dtype=np.float32)
    Wo = np.asarray(L0_attn_Wo, dtype=np.float32)
    bq = np.asarray(L0_attn_bq, dtype=np.float32)
    bk = np.asarray(L0_attn_bk, dtype=np.float32)
    bv = np.asarray(L0_attn_bv, dtype=np.float32)
    bo = np.asarray(L0_attn_bo, dtype=np.float32)
    gw = np.asarray(L0_ffn_gate_weight, dtype=np.float32)
    guw = np.asarray(L0_ffn_gate_up_weights, dtype=np.float32)
    dnw = np.asarray(L0_ffn_down_weights, dtype=np.float32)
    sgu = np.asarray(L0_ffn_shared_gate_up, dtype=np.float32)
    sdn = np.asarray(L0_ffn_shared_down, dtype=np.float32)

    j = np.arange(64, dtype=np.float64)
    theta = 10000.0 ** (-2.0 * j / D)
    tpos = np.arange(1, T + 1, dtype=np.float64)
    ang = tpos[None, :] * np.concatenate([theta, theta])[:, None]
    cosT = C(np.cos(ang).astype(np.float32))
    sinS = np.sin(ang).astype(np.float32)
    sinS[0:64, :] *= -1.0
    sinS = C(sinS)
    ii = np.arange(128)
    maskT = C(np.where(ii[:, None] <= ii[None, :], 0.0, NMASK).astype(np.float32))
    import ml_dtypes
    identbf = np.eye(128, dtype=np.float32).astype(mybir.dt.np(BF16))

    per_par = {}
    for par in range(2):
        hsl = slice(par * HPC * 128, (par + 1) * HPC * 128)
        wq_t = C(Wq.T[:, hsl].reshape(EC, 128, HPC, 128).transpose(1, 0, 2, 3))
        wk_t = C(Wk.T[:, hsl].reshape(EC, 128, HPC, 128).transpose(1, 0, 2, 3))
        wv_t = C(Wv.T[:, hsl].reshape(EC, 128, HPC * 128).transpose(1, 0, 2))
        bq_c = C(bq[hsl].reshape(HPC, 128).T)
        bk_c = C(bk[hsl].reshape(HPC, 128).T)
        bv_r = C(bv[hsl][None, :])
        per_par[par] = dict(wq=wq_t, wk=wk_t, wvT=wv_t, bq_c=bq_c, bk_c=bk_c, bv_r=bv_r,
                            par0=np.full((128, 1), 1.0 - par, np.float32),
                            par1=np.full((128, 1), float(par), np.float32))

    woT = C(Wo.T.reshape(EC, 128, E))
    bo_r = C(bo[None, :])
    gatewT = C(gw.T.reshape(EC, 128, NEXP).transpose(1, 0, 2))
    guw_t = C(guw.reshape(NEXP, 2, FCH, 128, EC, 128).transpose(0, 2, 5, 1, 4, 3))
    dnw_t = C(dnw.transpose(0, 2, 1).reshape(NEXP, FCH, 128, E))
    g_all = sgu[0:2816]
    u_all = sgu[2816:]
    sgu_t = np.empty((2, FCH, 2, EC, 128, 128), np.float32)
    for X in range(2):
        for fi in range(FCH):
            gt = g_all[(X * FCH + fi) * 128:(X * FCH + fi + 1) * 128]
            ut = u_all[(X * FCH + fi) * 128:(X * FCH + fi + 1) * 128]
            sgu_t[X, fi, 0] = gt.T.reshape(EC, 128, 128)
            sgu_t[X, fi, 1] = ut.T.reshape(EC, 128, 128)
    sgu_t = C(sgu_t.transpose(0, 1, 4, 2, 3, 5))
    sdn_t = C(sdn.T.reshape(2, FCH, 128, E))

    shared_all = dict(woT=woT, bo_r=bo_r, cosT=cosT, sinS=sinS,
                      maskT=maskT, identbf=identbf, gatewT=gatewT, guw=guw_t,
                      dnw=dnw_t, sguw=sgu_t, sdnw=sdn_t)
    in_maps = []
    for core in range(8):
        b = core // 2
        par = core % 2
        m = dict(shared_all)
        m.update(per_par[par])
        m["xfull"] = C(x[b])
        m["xmine"] = C(x[b, par * MYT:(par + 1) * MYT])
        in_maps.append(m)
    return in_maps, x.dtype


def kernel(**inputs):
    in_maps, xdt = _host_prep(**inputs)
    nc = build_program()
    res = run_bass_kernel_spmd(nc, in_maps, core_ids=list(range(8)))
    outf = np.empty((B, T, E), dtype=np.float32)
    for core in range(8):
        b, par = core // 2, core % 2
        outf[b, par * MYT:(par + 1) * MYT] = res.results[core]["out"]
    return outf


# version 08
# speedup vs baseline: 1.1046x; 1.0011x over previous
import math
import contextlib
import numpy as np
import concourse.bass as bass
import concourse.mybir as mybir
from concourse.bass_utils import run_bass_kernel_spmd

F32 = mybir.dt.float32
BF16 = mybir.dt.bfloat16
AF = mybir.ActivationFunctionType
ALU = mybir.AluOpType
AX = mybir.AxisListType.X

B, T, E, H, D = 4, 4096, 2048, 16, 128
HPC = 8
EC = E // 128
TCH = T // 128
TGR = T // 512
MYT = 2048
MTC = MYT // 128
NEXP = 16
FCH = 11
EPS = 1e-5
NMASK = -60000.0
RSCALE = 1.0 / math.sqrt(D)

ENGINES = ("tensor", "vector", "scalar", "gpsimd", "sync")


class Sched:

    def __init__(self, nc, ctx):
        self.nc = nc
        self.sems = {}
        self.cnt = {}
        self.q = {e: [] for e in ENGINES}
        self.ctx = ctx

    def sem(self, name):
        h = self.ctx.enter_context(self.nc.semaphore(name))
        self.sems[name] = h
        self.cnt[name] = 0

    def op(self, eng, builder, waits=(), inc=None, inc_n=1):
        after = None
        waits = list(waits)
        if inc is not None:
            if inc_n == 16:
                waits.append((inc, self.cnt[inc]))
            self.cnt[inc] += inc_n
            after = self.cnt[inc]
        self.q[eng].append((tuple(waits), builder, inc, inc_n))
        return after

    def replay(self, block):
        for engname in ENGINES:
            ops = self.q[engname]
            if not ops:
                continue

            def body(eh, ops=ops):
                for waits, builder, inc, inc_n in ops:
                    for s, v in waits:
                        if v > 0:
                            eh.wait_ge(self.sems[s], v)
                    ins = builder(eh)
                    if inc is not None and ins is not None:
                        ins.then_inc(self.sems[inc], inc_n)

            getattr(block, engname)(body)
        self.q = {e: [] for e in ENGINES}


def build_program(n_exp=18, dump_scr=False, dump_ag=False):
    nc = bass.Bass("TRN2", target_bir_lowering=False, debug=False)
    dt = nc.dram_tensor

    xfull = dt("xfull", [T, E], F32, kind="ExternalInput").ap()
    xmine = dt("xmine", [MYT, E], F32, kind="ExternalInput").ap()
    par0 = dt("par0", [128, 1], F32, kind="ExternalInput").ap()
    par1 = dt("par1", [128, 1], F32, kind="ExternalInput").ap()
    wq_d = dt("wq", [128, EC, HPC, 128], F32, kind="ExternalInput").ap()
    wk_d = dt("wk", [128, EC, HPC, 128], F32, kind="ExternalInput").ap()
    wv_d = dt("wvT", [128, EC, HPC * 128], F32, kind="ExternalInput").ap()
    wo_d = dt("woT", [EC, 128, E], F32, kind="ExternalInput").ap()
    bq_d = dt("bq_c", [128, HPC], F32, kind="ExternalInput").ap()
    bk_d = dt("bk_c", [128, HPC], F32, kind="ExternalInput").ap()
    bv_d = dt("bv_r", [1, HPC * 128], F32, kind="ExternalInput").ap()
    bo_d = dt("bo_r", [1, E], F32, kind="ExternalInput").ap()
    cos_d = dt("cosT", [128, T], F32, kind="ExternalInput").ap()
    sin_d = dt("sinS", [128, T], F32, kind="ExternalInput").ap()
    maskT_d = dt("maskT", [128, 128], F32, kind="ExternalInput").ap()
    identbf_d = dt("identbf", [128, 128], BF16, kind="ExternalInput").ap()
    gate_d = dt("gatewT", [128, EC, NEXP], F32, kind="ExternalInput").ap()
    guw_d = dt("guw", [NEXP, FCH, 128, 2, EC, 128], F32, kind="ExternalInput").ap()
    dnw_d = dt("dnw", [NEXP, FCH, 128, E], F32, kind="ExternalInput").ap()
    sgu_d = dt("sguw", [2, FCH, 128, 2, EC, 128], F32, kind="ExternalInput").ap()
    sdn_d = dt("sdnw", [2, FCH, 128, E], F32, kind="ExternalInput").ap()
    out = dt("out", [MYT, E], F32, kind="ExternalOutput").ap()
    if dump_scr:
        dmp_k = dt("dmp_k", [HPC, 128, T], BF16, kind="ExternalOutput").ap()
        dmp_q = dt("dmp_q", [HPC, 128, T], BF16, kind="ExternalOutput").ap()
        dmp_v = dt("dmp_v", [HPC, TCH, 128, 128], BF16, kind="ExternalOutput").ap()
    if dump_ag:
        dmp_ag = dt("dmp_ag", [2 * HPC * 128, T], BF16, kind="ExternalOutput").ap()

    kscr = dt("kscr", [HPC, 128, T], BF16, kind="Internal").ap()
    qscr = dt("qscr", [HPC, 128, T], BF16, kind="Internal").ap()
    vscr = dt("vscr", [TCH, 128, HPC * 128], BF16, kind="Internal").ap()
    ag_i1 = dt("ag_i1", [HPC * 64, T], BF16, kind="Internal").ap()
    ag_i2 = dt("ag_i2", [HPC * 64, T], BF16, kind="Internal").ap()
    ag_o1 = dt("ag_o1", [HPC * 128, T], BF16, kind="Internal").ap()
    ag_o2 = dt("ag_o2", [HPC * 128, T], BF16, kind="Internal").ap()

    with contextlib.ExitStack() as ctx:
        S = Sched(nc, ctx)
        for s in ["s_x", "s_w", "s_wg", "s_tr", "s_pe", "s_ve", "s_ac", "s_gp", "s_kv",
                  "s_kvr", "s_fw", "s_dn", "s_cc", "s_y", "s_out", "s_acc", "s_xm", "s_dmp"]:
            S.sem(s)
        sbuf = lambda n, sh, d_: ctx.enter_context(nc.sbuf_tensor(n, sh, d_))

        ones128 = sbuf("ones128_s", [128, 128], F32)
        onesbf = sbuf("onesbf_s", [128, 128], BF16)
        maskT = sbuf("maskT_s", [128, 128], F32)
        identbf = sbuf("identbf_s", [128, 128], BF16)
        par0c = sbuf("par0c_s", [128, 1], F32)
        par1c = sbuf("par1c_s", [128, 1], F32)
        eps_t = sbuf("eps_t", [128, 1], F32)
        rowbuf = sbuf("rowbuf_s", [1, 1024], F32)

        block = ctx.enter_context(nc.Block())

        for apd, dst in [(maskT_d, maskT), (identbf_d, identbf),
                         (par0, par0c), (par1, par1c)]:
            S.op("sync", lambda sy, a=apd, d=dst: sy.dma_start(out=d[:], in_=a),
                 inc="s_w", inc_n=16)
        const_w = S.cnt["s_w"]
        S.op("vector", lambda v: v.memset(ones128[:], 1.0), inc="s_ve")
        S.op("vector", lambda v: v.memset(eps_t[:], EPS), inc="s_ve")
        init_ve = S.op("vector", lambda v: v.memset(onesbf[:], 1.0), inc="s_ve")

        with contextlib.ExitStack() as c1:
            sb1 = lambda n, sh, d_: c1.enter_context(nc.sbuf_tensor(n, sh, d_))
            ps1 = lambda n, sh: c1.enter_context(nc.psum_tensor(n, sh, F32))
            cosg = [sb1(f"cosg{i}", [128, 1024], F32) for i in range(2)]
            sing = [sb1(f"sing{i}", [128, 1024], F32) for i in range(2)]
            wq_sb = sb1("wq_sb", [128, EC * HPC * 128], BF16)
            wk_sb = sb1("wk_sb", [128, EC * HPC * 128], BF16)
            wv_sb = sb1("wv_sb", [128, EC * 1024], BF16)
            x_t = [sb1(f"x_t{i}", [128, E], F32) for i in range(2)]
            xln = [sb1(f"xln{i}", [128, E], BF16) for i in range(2)]
            xlnT = [sb1(f"xlnT{i}", [128, EC * 512], BF16) for i in range(2)]
            kst = [sb1(f"kst{i}", [128, 1024], F32) for i in range(2)]
            krot = [sb1(f"krot{i}", [128, 1024], F32) for i in range(2)]
            kbf = [sb1(f"kbf{i}", [128, 1024], BF16) for i in range(2)]
            vbf = [sb1(f"vbf{i}", [128, 1024], BF16) for i in range(2)]
            bqs = sb1("bqs", [128, HPC], F32)
            bks = sb1("bks", [128, HPC], F32)
            bvb = sb1("bvb", [128, 1024], F32)
            sqs = sb1("sqs", [128, 1024], F32)
            st1 = sb1("st1", [128, 8], F32)
            psP = [ps1(f"psP{i}", [128, 512]) for i in range(4)]
            psV = [ps1(f"psV{i}", [128, 512]) for i in range(2)]
            psTr = [c1.enter_context(nc.psum_tensor(f"psTr{i}", [128, 512], BF16))
                    for i in range(2)]

            S.op("sync", lambda sy: sy.dma_start(out=bqs[:], in_=bq_d), inc="s_w", inc_n=16)
            bk_w = S.op("sync", lambda sy: sy.dma_start(out=bks[:], in_=bk_d), inc="s_w", inc_n=16)
            S.op("sync", lambda sy: sy.dma_start(out=rowbuf[0:1, 0:1024], in_=bv_d),
                 inc="s_w", inc_n=16)
    
            wk_done = wq_done = 0
            for src_d, dst_sb in ((wk_d, wk_sb), (wq_d, wq_sb)):
                for eh in range(2):
                    last = S.op("gpsimd", lambda g, src_d=src_d, dst_sb=dst_sb, eh=eh: g.dma_start(
                        out=dst_sb[:, eh * 8192:(eh + 1) * 8192]
                        .rearrange("p (e f c) -> p e f c", e=EC // 2, f=HPC),
                        in_=src_d[:, eh * 8:(eh + 1) * 8]),
                        inc="s_wg", inc_n=16)
                if src_d is wk_d:
                    wk_done = last
                else:
                    wq_done = last
            S.op("gpsimd", lambda g: g.dma_start(out=wv_sb[:].rearrange("p (e c) -> p e c", e=EC), in_=wv_d),
                 inc="s_wg", inc_n=16)
            w_done = S.cnt["s_w"]
            wg_done = S.cnt["s_wg"]
            bks_ve = S.op("vector", lambda v: v.tensor_scalar_mul(bks[:], bks[:], RSCALE),
                          waits=[("s_w", bk_w)], inc="s_ve")
            bvb_pe = 0
            for nn in range(2):
                bvb_pe = S.op("tensor", lambda t, nn=nn: t.matmul(
                    psV[nn][:], ones128[0:1, :],
                    rowbuf[0:1, nn * 512:(nn + 1) * 512], start=True, stop=True),
                    waits=[("s_w", w_done), ("s_ve", init_ve)], inc="s_pe")
            bvb_ve = 0
            for nn in range(2):
                bvb_ve = S.op("vector", lambda v, nn=nn: v.tensor_copy(
                    bvb[:, nn * 512:(nn + 1) * 512], psV[nn][:]),
                    waits=[("s_pe", bvb_pe)], inc="s_ve")

            ln_done = [0] * TCH
            trp_done = [0] * TCH
            trev_done = [0] * TCH
            pr_done = [0] * TGR
            evac_free = {}
            tr_bank_free = {}
            kq_c3 = {}
            kq_c5 = {}
            kbf_free = {}
            vbf_free = [0, 0]
            vps_free = {0: bvb_ve, 1: bvb_ve}
            kq_uses = 0
            rstd_prev = 0
            r4_prev = 0
            rope_read = [(0, 0), (0, 0)]
            cs_ld = [0] * TGR
            for tc in range(TCH):
                g = tc // 4
                if tc % 4 == 0:
                    rve, rgp = rope_read[g % 2]
                    waits = []
                    if rve:
                        waits += [("s_ve", rve), ("s_gp", rgp)]
                    for nn in range(2):
                        S.op("sync", lambda sy, g=g, nn=nn: sy.dma_start(
                            out=cosg[g % 2][:, nn * 512:(nn + 1) * 512],
                            in_=cos_d[:, g * 512:(g + 1) * 512]),
                            waits=waits if nn == 0 else (), inc="s_x", inc_n=16)
                        cs_ld[g] = S.op("sync", lambda sy, g=g, nn=nn: sy.dma_start(
                            out=sing[g % 2][:, nn * 512:(nn + 1) * 512],
                            in_=sin_d[:, g * 512:(g + 1) * 512]),
                            inc="s_x", inc_n=16)
                waits = [("s_ve", ln_done[tc - 2])] if tc >= 2 else []
                x_ld = S.op("sync", lambda sy, tc=tc: sy.dma_start(
                    out=x_t[tc % 2][:], in_=xfull[tc * 128:(tc + 1) * 128, :]),
                    waits=waits, inc="s_x", inc_n=16)
                waits = [("s_x", x_ld)]
                if r4_prev:
                    waits.append(("s_ve", r4_prev))
                S.op("scalar", lambda s, tc=tc: s.activation(
                    sqs[:], x_t[tc % 2][:, 0:1024], AF.Square, accum_out=st1[:, 4:5]),
                    waits=waits, inc="s_ac")
                sq2 = S.op("scalar", lambda s, tc=tc: s.activation(
                    sqs[:], x_t[tc % 2][:, 1024:2048], AF.Square, accum_out=st1[:, 5:6]),
                    inc="s_ac")
                r1 = S.op("vector", lambda v, tc=tc: v.reduce_sum(
                    st1[:, 0:1], x_t[tc % 2][:], AX),
                    waits=[("s_x", x_ld)], inc="s_ve")
                r2 = S.op("vector", lambda v: v.tensor_scalar_mul(st1[:, 0:1], st1[:, 0:1], 1.0 / E),
                          waits=[("s_ve", r1)], inc="s_ve")
                S.op("vector", lambda v: v.tensor_mul(st1[:, 1:2], st1[:, 0:1], st1[:, 0:1]),
                     waits=[("s_ve", r2)], inc="s_ve")
                waits = [("s_ac", sq2)]
                if rstd_prev:
                    waits.append(("s_ac", rstd_prev))
                r4_prev = S.op("vector", lambda v: v.tensor_add(
                    st1[:, 2:3], st1[:, 4:5], st1[:, 5:6]),
                    waits=waits, inc="s_ve")
                r4b = S.op("vector", lambda v: v.tensor_scalar_mul(
                    st1[:, 2:3], st1[:, 2:3], 1.0 / E),
                    waits=[("s_ve", r4_prev)], inc="s_ve")
                r5 = S.op("vector", lambda v: v.tensor_sub(
                    st1[:, 2:3], st1[:, 2:3], st1[:, 1:2]),
                    waits=[("s_ve", r4b)], inc="s_ve")
                a1 = S.op("scalar", lambda s: s.activation(
                    st1[:, 3:4], st1[:, 2:3], AF.Ln, bias=eps_t[:, 0:1]),
                    waits=[("s_ve", r5)], inc="s_ac")
                rstd_prev = S.op("scalar", lambda s: s.activation(
                    st1[:, 3:4], st1[:, 3:4], AF.Exp, scale=-0.5),
                    waits=[("s_ac", a1)], inc="s_ac")
                waits = [("s_ac", rstd_prev)]
                if tc >= 2:
                    waits.append(("s_pe", trp_done[tc - 2]))
                ln_done[tc] = S.op("vector", lambda v, tc=tc: v.tensor_scalar(
                    xln[tc % 2][:], x_t[tc % 2][:], st1[:, 0:1], st1[:, 3:4],
                    ALU.subtract, ALU.mult),
                    waits=waits, inc="s_ve")
                for e in range(EC):
                    b = (e // 4) % 2
                    waits = []
                    if e == 0:
                        waits.append(("s_ve", ln_done[tc]))
                    if e % 4 == 0 and tr_bank_free.get(b):
                        waits.append(("s_ac", tr_bank_free[b]))
                    trp_done[tc] = S.op("tensor", lambda t, tc=tc, e=e, b=b: t.transpose(
                        psTr[b][:, (e % 4) * 128:(e % 4 + 1) * 128],
                        xln[tc % 2][:, e * 128:(e + 1) * 128], identbf[:]),
                        waits=waits, inc="s_pe")
                    if e % 4 == 3:
                        waits = [("s_pe", trp_done[tc])]
                        if e == 3 and g >= 2:
                            waits.append(("s_pe", pr_done[g - 2]))
                        tr_bank_free[b] = S.op("scalar", lambda s, tc=tc, e=e, b=b, g=g:
                                               s.copy(xlnT[g % 2]
                                                      .rearrange("p (e c) -> p e c", e=EC)
                                                      [:, e - 3:e + 1, (tc % 4) * 128:(tc % 4) * 128 + 128],
                                                      psTr[b][:].rearrange("p (e c) -> p e c", e=4)),
                                               waits=waits, inc="s_ac")
                trev_done[tc] = S.cnt["s_ac"]

                if tc % 4 != 3:
                    continue

                xg = xlnT[g % 2]
                base_waits = [("s_ac", trev_done[tc])]
                if g == 0:
                    base_waits += [("s_w", w_done), ("s_wg", wk_done),
                                   ("s_ve", max(bvb_ve, bks_ve))]

                for h in range(HPC):
                    bankK = (2 * h) % 4
                    bankQ = (2 * h + 1) % 4
                    stg = h % 2
                    waits = list(base_waits if h == 0 else [])
                    if g == 0 and h == 0:
                        pass
                    if evac_free.get(bankK):
                        waits.append(("s_ac", evac_free[bankK]))

                    def mmk(t, h=h, bankK=bankK, xg=xg):
                        for e in range(EC):
                            mm = t.matmul(psP[bankK][:],
                                          wk_sb[:, (e * HPC + h) * 128:(e * HPC + h) * 128 + 128],
                                          xg[:, e * 512:(e + 1) * 512],
                                          start=(e == 0), stop=(e == EC - 1))
                        return mm
                    mmk_pe = S.op("tensor", mmk, waits=waits, inc="s_pe")
                    waits = []
                    if g == 0 and h == 0:
                        waits.append(("s_wg", wq_done))
                    if evac_free.get(bankQ):
                        waits.append(("s_ac", evac_free[bankQ]))

                    def mmq(t, h=h, bankQ=bankQ, xg=xg):
                        for e in range(EC):
                            mm = t.matmul(psP[bankQ][:],
                                          wq_sb[:, (e * HPC + h) * 128:(e * HPC + h) * 128 + 128],
                                          xg[:, e * 512:(e + 1) * 512],
                                          start=(e == 0), stop=(e == EC - 1))
                        return mm
                    mmq_pe = S.op("tensor", mmq, waits=waits, inc="s_pe")

                    waits = [("s_pe", mmk_pe)]
                    if kq_c3.get(stg):
                        waits.append(("s_ve", kq_c3[stg]))
                    evK = S.op("scalar", lambda s, bankK=bankK, stg=stg, h=h:
                               s.activation(kst[stg][:, 0:512], psP[bankK][:], AF.Identity,
                                            bias=bks[:, h:h + 1], scale=RSCALE),
                               waits=waits, inc="s_ac")
                    evac_free[bankK] = evK
                    evQ = S.op("scalar", lambda s, bankQ=bankQ, stg=stg, h=h:
                               s.activation(kst[stg][:, 512:1024], psP[bankQ][:], AF.Identity,
                                            bias=bqs[:, h:h + 1], scale=1.0),
                               waits=[("s_pe", mmq_pe)], inc="s_ac")
                    evac_free[bankQ] = evQ
                    waits = [("s_ac", evQ)]
                    if kq_c5.get(stg):
                        waits.append(("s_ve", kq_c5[stg]))
                    S.op("scalar", lambda s, stg=stg: s.copy(
                        krot[stg][0:64, :], kst[stg][64:128, :]), waits=waits, inc="s_ac")
                    c2 = S.op("scalar", lambda s, stg=stg: s.copy(
                        krot[stg][64:128, :], kst[stg][0:64, :]), inc="s_ac")
                    c3 = S.op("vector", lambda v, stg=stg, g=g: v.tensor_mul(
                        kst[stg][:], kst[stg][:], cosg[g % 2][:]),
                        waits=[("s_ac", c2), ("s_x", cs_ld[g])], inc="s_ve")
                    c4 = S.op("gpsimd", lambda gq, stg=stg, g=g: gq.tensor_mul(
                        krot[stg][:], krot[stg][:], sing[g % 2][:]),
                        waits=[("s_ac", c2), ("s_x", cs_ld[g])], inc="s_gp")
                    waits = [("s_ve", c3), ("s_gp", c4)]
                    if kbf_free.get(stg):
                        waits.append(("s_kv", kbf_free[stg]))
                    c5 = S.op("vector", lambda v, stg=stg: v.tensor_add(
                        kbf[stg][:], kst[stg][:], krot[stg][:]),
                        waits=waits, inc="s_ve")
                    kq_c3[stg] = c5
                    kq_c5[stg] = c5
                    S.op("gpsimd", lambda gq, h=h, g=g, stg=stg:
                         gq.dma_start(out=kscr[h, :, g * 512:(g + 1) * 512],
                                      in_=kbf[stg][:, 0:512]),
                         waits=[("s_ve", c5)], inc="s_kv", inc_n=16)
                    kbf_free[stg] = S.op("gpsimd", lambda gq, h=h, g=g, stg=stg:
                                         gq.dma_start(out=qscr[h, :, g * 512:(g + 1) * 512],
                                                      in_=kbf[stg][:, 512:1024]),
                                         inc="s_kv", inc_n=16)
                    kq_uses += 1
                rope_read[g % 2] = (c3, c4)

                for tsub in range(4):
                    tci = g * 4 + tsub
                    vmm = [0, 0]
                    for nn in range(2):
                        waits = list(base_waits if (tsub == 0 and nn == 0) else [])
                        if g == 0 and tsub == 0 and nn == 0:
                            waits.append(("s_wg", wg_done))
                        if vps_free[nn]:
                            waits.append(("s_ve", vps_free[nn]))

                        def mmv(t, nn=nn, tsub=tsub, xg=xg):
                            for e in range(EC):
                                mm = t.matmul(psV[nn][:],
                                              xg[:, e * 512 + tsub * 128:e * 512 + tsub * 128 + 128],
                                              wv_sb[:, e * 1024 + nn * 512:e * 1024 + (nn + 1) * 512],
                                              start=(e == 0), stop=(e == EC - 1))
                            return mm
                        vmm[nn] = S.op("tensor", mmv, waits=waits, inc="s_pe")
                    vev = 0
                    for nn in range(2):
                        waits = [("s_pe", vmm[nn])]
                        if nn == 0 and vbf_free[tci % 2]:
                            waits.append(("s_kv", vbf_free[tci % 2]))
                        vev = S.op("vector", lambda v, nn=nn, tci=tci: v.tensor_add(
                            vbf[tci % 2][:, nn * 512:(nn + 1) * 512], psV[nn][:],
                            bvb[:, nn * 512:(nn + 1) * 512]), waits=waits, inc="s_ve")
                        vps_free[nn] = vev
                    vbf_free[tci % 2] = S.op("gpsimd", lambda gq, tci=tci: gq.dma_start(
                        out=vscr[tci], in_=vbf[tci % 2][:]),
                        waits=[("s_ve", vev)], inc="s_kv", inc_n=16)
                pr_done[g] = S.cnt["s_pe"]

            p1_kv = S.cnt["s_kv"]
            p1_tr = S.cnt["s_tr"]
            S.replay(block)

        with contextlib.ExitStack() as cB:

            with contextlib.ExitStack() as c2:
                sb2 = lambda n, sh, d_: c2.enter_context(nc.sbuf_tensor(n, sh, d_))
                ps2 = lambda n, sh: c2.enter_context(nc.psum_tensor(n, sh, F32))
                kT2 = [sb2(f"kT2_{i}", [128, T], BF16) for i in range(2)]
                qT2 = [sb2(f"qT2_{i}", [128, T], BF16) for i in range(2)]
                vt2 = [sb2(f"vt2_{i}", [128, T], BF16) for i in range(2)]
                y_sb = sb2("y_sb", [128, HPC * T], BF16)
                pT_bf = [sb2(f"pT_bf{i}", [128, 512], BF16) for i in range(4)]
                rlb = [sb2(f"rlb{i}", [128, 512], F32) for i in range(2)]
                psT = [ps2(f"psT{i}", [128, 512]) for i in range(4)]
                psO = [ps2(f"psO{i}", [128, 512]) for i in range(2)]
                psL = [ps2(f"psL{i}", [128, 512]) for i in range(2)]

                head_loads = [0] * HPC
                head_pe_done = [0] * HPC
                exp_free = {}
                pv_free = {}
                grp_epi = {0: 0, 1: 0}
                gidx = 0
                y_dma = [0] * HPC

                def emit_head_loads(h):
                    waits = [("s_kv", p1_kv)]
                    if h >= 2:
                        waits.append(("s_pe", head_pe_done[h - 2]))
                    S.op("sync", lambda sy, h=h: sy.dma_start(out=kT2[h % 2][:], in_=kscr[h]),
                         waits=waits, inc="s_kvr", inc_n=16)
                    S.op("sync", lambda sy, h=h: sy.dma_start(out=qT2[h % 2][:], in_=qscr[h]),
                         inc="s_kvr", inc_n=16)
                    head_loads[h] = S.op("sync", lambda sy, h=h: sy.dma_start(
                        out=vt2[h % 2][:].rearrange("p (a c) -> p a c", a=TCH),
                        in_=vscr[:, :, h * 128:(h + 1) * 128].rearrange("a p c -> p a c")),
                        inc="s_kvr", inc_n=16)

                emit_head_loads(0)
                for h in range(HPC):
                    if h + 1 < HPC:
                        emit_head_loads(h + 1)
                    kk, qq, vv = kT2[h % 2], qT2[h % 2], vt2[h % 2]

                    for g2 in range(TGR):
                        nkt = 4 * g2 + 4
                        pb = gidx % 2
                        gidx += 1
                        exps = [0] * nkt
                        lastL = [0]
                        lastV = [0]

                        def emit_LV(kt, g2=g2, nkt=nkt, pb=pb, vv=vv,
                                    exps=exps, lastL=lastL, lastV=lastV):
                            qoff = max(0, (kt - 4 * g2) * 128)
                            buf = kt % 4
                            waits = [("s_ac", exps[kt])]
                            if kt == 0 and grp_epi[pb]:
                                waits.append(("s_ve", grp_epi[pb]))
                            lastL[0] = S.op("tensor", lambda t, kt=kt, qoff=qoff, buf=buf:
                                            t.matmul(psL[pb][:, qoff:512], onesbf[:],
                                                     pT_bf[buf][:, qoff:512],
                                                     start=(kt == 0), stop=(kt == nkt - 1)),
                                            waits=waits, inc="s_pe")
                            lastV[0] = S.op("tensor", lambda t, kt=kt, qoff=qoff, buf=buf:
                                            t.matmul(psO[pb][:, qoff:512],
                                                     vv[:, kt * 128:(kt + 1) * 128],
                                                     pT_bf[buf][:, qoff:512],
                                                     start=(kt == 0), stop=(kt == nkt - 1)),
                                            inc="s_pe")
                            pv_free[buf] = lastV[0]

                        for kt in range(nkt):
                            qoff = max(0, (kt - 4 * g2) * 128)
                            bank = kt % 4
                            waits = [("s_kvr", head_loads[h])]
                            if exp_free.get(bank):
                                waits.append(("s_ac", exp_free[bank]))
                            mmS = S.op("tensor", lambda t, kt=kt, qoff=qoff, bank=bank,
                                       g2=g2, kk=kk, qq=qq: t.matmul(
                                           psT[bank][:, qoff:512],
                                           kk[:, kt * 128:(kt + 1) * 128],
                                           qq[:, g2 * 512 + qoff:(g2 + 1) * 512],
                                           start=True, stop=True),
                                       waits=waits, inc="s_pe")
                            dep = [("s_pe", mmS)]
                            if kt >= 4 * g2:
                                mk = S.op("vector", lambda v, bank=bank, qoff=qoff: v.tensor_add(
                                    psT[bank][:, qoff:qoff + 128],
                                    psT[bank][:, qoff:qoff + 128], maskT[:]),
                                    waits=dep, inc="s_ve")
                                dep = [("s_ve", mk)]
                            waits = list(dep)
                            if pv_free.get(bank):
                                waits.append(("s_pe", pv_free[bank]))
                            exps[kt] = S.op("scalar", lambda s, bank=bank, qoff=qoff:
                                            s.activation(pT_bf[bank][:, qoff:512],
                                                         psT[bank][:, qoff:512], AF.Exp),
                                            waits=waits, inc="s_ac")
                            exp_free[bank] = exps[kt]
                            if kt >= 2:
                                emit_LV(kt - 2)
                        emit_LV(nkt - 2)
                        emit_LV(nkt - 1)

                        waits = [("s_pe", lastL[0])]
                        if grp_epi[pb]:
                            waits.append(("s_ve", grp_epi[pb]))
                        rc = S.op("vector", lambda v, pb=pb: v.reciprocal(
                            rlb[pb][:], psL[pb][:]), waits=waits, inc="s_ve")
                        grp_epi[pb] = S.op("vector", lambda v, h=h, g2=g2, pb=pb: v.tensor_mul(
                            y_sb[:, h * T + g2 * 512:h * T + (g2 + 1) * 512],
                            psO[pb][:], rlb[pb][:]),
                            waits=[("s_pe", lastV[0]), ("s_ve", rc)], inc="s_ve")

                    head_pe_done[h] = S.cnt["s_pe"]
                    agt = ag_i1 if h < 4 else ag_i2
                    hr = h % 4
                    y_dma[h] = S.op("sync", lambda sy, h=h, agt=agt, hr=hr: sy.dma_start(
                        out=agt[hr * 128:(hr + 1) * 128, :], in_=y_sb[:, h * T:(h + 1) * T]),
                        waits=[("s_ve", grp_epi[(gidx - 1) % 2])], inc="s_y", inc_n=16)
                    if h == 3:
                        S.op("gpsimd", lambda g: g.collective_compute(
                            "AllGather", ALU.bypass,
                            replica_groups=[[0, 1], [2, 3], [4, 5], [6, 7]],
                            ins=[ag_i1.opt()], outs=[ag_o1.opt()]),
                            waits=[("s_y", y_dma[3])], inc="s_cc")

                cc_done = S.op("gpsimd", lambda g: g.collective_compute(
                    "AllGather", ALU.bypass,
                    replica_groups=[[0, 1], [2, 3], [4, 5], [6, 7]],
                    ins=[ag_i2.opt()], outs=[ag_o2.opt()]),
                    waits=[("s_y", y_dma[HPC - 1])], inc="s_cc")
                if dump_scr:
                    for src, dst in [(kscr, dmp_k), (qscr, dmp_q)]:
                        S.op("sync", lambda sy, s=src, d=dst: sy.dma_start(out=d, in_=s),
                             waits=[("s_kv", p1_kv)], inc="s_dmp", inc_n=16)
                    S.op("sync", lambda sy: sy.dma_start(out=dmp_v, in_=vscr),
                         waits=[("s_kv", p1_kv)], inc="s_dmp", inc_n=16)
                p2_ve = S.cnt["s_ve"]
                p2_ac = S.cnt["s_ac"]
                S.replay(block)

            with contextlib.ExitStack() as cC:
                sbC = lambda n, sh, d_: cC.enter_context(nc.sbuf_tensor(n, sh, d_))
                xln2T = sbC("xln2T", [128, EC * MYT], BF16)
                w_all = sbC("w_all", [128, MTC * NEXP], F32)
                gate_sb = sbC("gate_sb", [128, EC * NEXP], BF16)

                with contextlib.ExitStack() as c3:
                    sb3 = lambda n, sh, d_: c3.enter_context(nc.sbuf_tensor(n, sh, d_))
                    ps3 = lambda n, sh: c3.enter_context(nc.psum_tensor(n, sh, F32))
                    woT_sb = sb3("woT_sb", [128, EC * E], BF16)
                    bob = sb3("bob", [128, E], F32)
                    xm = sb3("xm", [128, E], F32)
                    y2ch = [sb3(f"y2ch{i}", [128, EC * 256], BF16) for i in range(2)]
                    y2bh = sb3("y2bh", [128, EC * 256], BF16)
                    x1 = [sb3(f"x1_{i}", [128, E], F32) for i in range(2)]
                    xln2 = [sb3(f"xln2_{i}", [128, E], BF16) for i in range(2)]
                    sqs3 = sb3("sqs3", [128, 1024], F32)
                    st3 = sb3("st3", [128, 8], F32)
                    sc4 = sb3("sc4", [128, 4 * NEXP], F32)
                    t4 = sb3("t4", [128, 4], F32)
                    psW = [ps3(f"psW{i}", [128, 512]) for i in range(2)]
                    psG = ps3("psG", [128, 512])
                    psTr3 = [c3.enter_context(nc.psum_tensor(f"psTr3_{i}", [128, 512], BF16))
                             for i in range(2)]

                    for _eng in ("tensor", "vector", "scalar"):
                        S.op(_eng, lambda e: None, waits=[("s_cc", cc_done)])
                    wo_w = S.op("gpsimd", lambda g: g.dma_start(
                        out=woT_sb[:].rearrange("p (e c) -> p e c", e=EC),
                        in_=wo_d.rearrange("e p c -> p e c")),
                        waits=[("s_y", S.cnt["s_y"]), ("s_ve", p2_ve), ("s_ac", p2_ac)],
                        inc="s_wg", inc_n=16)
                    gate_w = S.op("gpsimd", lambda g: g.dma_start(
                        out=gate_sb[:].rearrange("p (e c) -> p e c", e=EC), in_=gate_d),
                        waits=[("s_cc", cc_done)], inc="s_wg", inc_n=16)
                    bob_ve = 0
                    for jj in range(2):
                        bo_w = S.op("sync", lambda sy, jj=jj: sy.dma_start(
                            out=rowbuf[:], in_=bo_d[0:1, jj * 1024:(jj + 1) * 1024]),
                            waits=[("s_cc", cc_done)] + ([("s_pe", bmm)] if jj else []),
                            inc="s_w", inc_n=16)
                        for j2 in range(2):
                            j = jj * 2 + j2
                            bmm = S.op("tensor", lambda t, j2=j2: t.matmul(
                                psW[0][:, 0:512], ones128[0:1, :],
                                rowbuf[0:1, j2 * 512:(j2 + 1) * 512], start=True, stop=True),
                                waits=[("s_w", bo_w)] + ([("s_ve", bob_ve)] if bob_ve else []),
                                inc="s_pe")
                            bob_ve = S.op("vector", lambda v, j=j: v.tensor_copy(
                                bob[:, j * 512:(j + 1) * 512], psW[0][:]),
                                waits=[("s_pe", bmm)], inc="s_ve")

                    x1_free = [0, 0]
                    x1_out = [0, 0]
                    wo_rd = [0, 0]
                    sel_rd = 0
                    xln2_free = [0, 0]
                    psW_free = {0: bob_ve, 1: 0}
                    tr3_bank_free = {}
                    xm_free = 0
                    tr3_ev = [0] * MTC
                    trp3 = [0] * MTC
                    psG_free = 0
                    out_init = 0
                    rstd3_prev = 0
                    r43_prev = 0

                    for ch in range(MTC // 2):
                        lda = ldb = 0
                        for fb in range(4):
                            src, r0 = [(ag_o1, 0), (ag_o2, 0),
                                       (ag_o1, 512), (ag_o2, 512)][fb]
                            waits = [("s_cc", 2 if ch == 0 else (1 if fb % 2 == 0 else 2))]
                            if fb == 0 and wo_rd[ch % 2]:
                                waits.append(("s_pe", wo_rd[ch % 2]))
                            lda = S.op("sync", lambda sy, ch=ch, fb=fb, src=src, r0=r0:
                                       sy.dma_start(
                                           out=y2ch[ch % 2][:, fb * 1024:(fb + 1) * 1024]
                                           .rearrange("p (f c) -> p f c", f=4),
                                           in_=src[r0:r0 + 512, ch * 256:(ch + 1) * 256]
                                           .rearrange("(f p) c -> p f c", p=128)),
                                       waits=waits, inc="s_kvr", inc_n=16)
                            waits = [("s_cc", 2 if ch == 0 else (1 if fb % 2 == 0 else 2))]
                            if fb == 0 and sel_rd:
                                waits.append(("s_ve", sel_rd))
                            ldb = S.op("gpsimd", lambda s, ch=ch, fb=fb, src=src, r0=r0:
                                       s.dma_start(
                                           out=y2bh[:, fb * 1024:(fb + 1) * 1024]
                                           .rearrange("p (f c) -> p f c", f=4),
                                           in_=src[r0:r0 + 512, MYT + ch * 256:MYT + (ch + 1) * 256]
                                           .rearrange("(f p) c -> p f c", p=128)),
                                       waits=waits, inc="s_tr", inc_n=16)
                        s1v = S.op("vector", lambda v, ch=ch: v.tensor_scalar(
                            y2ch[ch % 2][:], y2ch[ch % 2][:], par0c[:, 0:1], None, ALU.mult),
                            waits=[("s_kvr", lda)], inc="s_ve")
                        sel_done = S.op("vector", lambda v, ch=ch: v.scalar_tensor_tensor(
                            y2ch[ch % 2][:], y2bh[:], par1c[:, 0:1], y2ch[ch % 2][:],
                            ALU.mult, ALU.add),
                            waits=[("s_ve", s1v), ("s_tr", ldb)], inc="s_ve")
                        sel_rd = sel_done

                        for tci in range(2):
                            tc = 2 * ch + tci
                            waits = [("s_cc", cc_done)] if tc == 0 else []
                            if xm_free:
                                waits.append(("s_ve", xm_free))
                            xml = S.op("sync", lambda sy, tc=tc: sy.dma_start(
                                out=xm[:], in_=xmine[tc * 128:(tc + 1) * 128, :]),
                                waits=waits, inc="s_xm", inc_n=16)

                            wo_mm = 0
                            for oc in range(4):
                                waits = [("s_ve", sel_done)]
                                if oc == 0 and tc == 0:
                                    waits.append(("s_wg", wo_w))
                                if psW_free[oc % 2]:
                                    waits.append(("s_ve", psW_free[oc % 2]))

                                def mmw(t, ch=ch, tci=tci, oc=oc):
                                    for f in range(EC):
                                        mm = t.matmul(psW[oc % 2][:],
                                                      y2ch[ch % 2][:, f * 256 + tci * 128:f * 256 + tci * 128 + 128],
                                                      woT_sb[:, f * E + oc * 512:f * E + (oc + 1) * 512],
                                                      start=(f == 0), stop=(f == EC - 1))
                                    return mm
                                wo_mm = S.op("tensor", mmw, waits=waits, inc="s_pe")
                                waits = [("s_pe", wo_mm)]
                                if oc == 0:
                                    if x1_free[tc % 2]:
                                        waits.append(("s_ve", x1_free[tc % 2]))
                                    if x1_out[tc % 2]:
                                        waits.append(("s_out", x1_out[tc % 2]))
                                ev = S.op("vector", lambda v, tc=tc, oc=oc: v.scalar_tensor_tensor(
                                    x1[tc % 2][:, oc * 512:(oc + 1) * 512], psW[oc % 2][:], 1.0,
                                    bob[:, oc * 512:(oc + 1) * 512], ALU.mult, ALU.add),
                                    waits=waits, inc="s_ve")
                                psW_free[oc % 2] = ev
                            wo_rd[ch % 2] = wo_mm
                            xm_free = S.op("vector", lambda v, tc=tc: v.tensor_add(
                                x1[tc % 2][:], x1[tc % 2][:], xm[:]),
                                waits=[("s_ve", psW_free[1]), ("s_ve", psW_free[0]),
                                       ("s_xm", xml)], inc="s_ve")
                            x1_done = xm_free
                            x1_out[tc % 2] = S.op("gpsimd", lambda g, tc=tc: g.dma_start(
                                out=out[tc * 128:(tc + 1) * 128, :], in_=x1[tc % 2][:]),
                                waits=[("s_ve", x1_done)], inc="s_out", inc_n=16)

                            waits = [("s_ve", x1_done)]
                            if r43_prev:
                                waits.append(("s_ve", r43_prev))
                            S.op("scalar", lambda s, tc=tc: s.activation(
                                sqs3[:], x1[tc % 2][:, 0:1024], AF.Square,
                                accum_out=st3[:, 4:5]),
                                waits=waits, inc="s_ac")
                            sq2 = S.op("scalar", lambda s, tc=tc: s.activation(
                                sqs3[:], x1[tc % 2][:, 1024:2048], AF.Square,
                                accum_out=st3[:, 5:6]),
                                inc="s_ac")
                            r1 = S.op("vector", lambda v, tc=tc: v.reduce_sum(
                                st3[:, 0:1], x1[tc % 2][:], AX),
                                waits=[("s_ve", x1_done)], inc="s_ve")
                            r2 = S.op("vector", lambda v: v.tensor_scalar_mul(
                                st3[:, 0:1], st3[:, 0:1], 1.0 / E),
                                waits=[("s_ve", r1)], inc="s_ve")
                            S.op("vector", lambda v: v.tensor_mul(
                                st3[:, 1:2], st3[:, 0:1], st3[:, 0:1]),
                                waits=[("s_ve", r2)], inc="s_ve")
                            waits = [("s_ac", sq2)]
                            if rstd3_prev:
                                waits.append(("s_ac", rstd3_prev))
                            r43_prev = S.op("vector", lambda v: v.tensor_add(
                                st3[:, 2:3], st3[:, 4:5], st3[:, 5:6]),
                                waits=waits, inc="s_ve")
                            r4b = S.op("vector", lambda v: v.tensor_scalar_mul(
                                st3[:, 2:3], st3[:, 2:3], 1.0 / E),
                                waits=[("s_ve", r43_prev)], inc="s_ve")
                            r5 = S.op("vector", lambda v: v.tensor_sub(
                                st3[:, 2:3], st3[:, 2:3], st3[:, 1:2]),
                                waits=[("s_ve", r4b)], inc="s_ve")
                            a1 = S.op("scalar", lambda s: s.activation(
                                st3[:, 3:4], st3[:, 2:3], AF.Ln, bias=eps_t[:, 0:1]),
                                waits=[("s_ve", r5)], inc="s_ac")
                            rstd3_prev = S.op("scalar", lambda s: s.activation(
                                st3[:, 3:4], st3[:, 3:4], AF.Exp, scale=-0.5),
                                waits=[("s_ac", a1)], inc="s_ac")
                            waits = [("s_ac", rstd3_prev)]
                            if xln2_free[tc % 2]:
                                waits.append(("s_pe", xln2_free[tc % 2]))
                            ln2_ap = S.op("vector", lambda v, tc=tc: v.tensor_scalar(
                                xln2[tc % 2][:], x1[tc % 2][:], st3[:, 0:1], st3[:, 3:4],
                                ALU.subtract, ALU.mult),
                                waits=waits, inc="s_ve")
                            x1_free[tc % 2] = ln2_ap
                            for e in range(EC):
                                b = (e // 4) % 2
                                waits = []
                                if e == 0:
                                    waits.append(("s_ve", ln2_ap))
                                if e % 4 == 0 and tr3_bank_free.get(b):
                                    waits.append(("s_ac", tr3_bank_free[b]))
                                trp3[tc] = S.op("tensor", lambda t, tc=tc, e=e, b=b: t.transpose(
                                    psTr3[b][:, (e % 4) * 128:(e % 4 + 1) * 128],
                                    xln2[tc % 2][:, e * 128:(e + 1) * 128], identbf[:]),
                                    waits=waits, inc="s_pe")
                                if e % 4 == 3:
                                    tr3_bank_free[b] = S.op(
                                        "scalar", lambda s, tc=tc, e=e, b=b: s.copy(
                                            xln2T.rearrange("p (e c) -> p e c", e=EC)
                                            [:, e - 3:e + 1, tc * 128:(tc + 1) * 128],
                                            psTr3[b][:].rearrange("p (e c) -> p e c", e=4)),
                                        waits=[("s_pe", trp3[tc])], inc="s_ac")
                            tr3_ev[tc] = S.cnt["s_ac"]
                            xln2_free[tc % 2] = trp3[tc]

                            waits = [("s_ac", tr3_ev[tc]), ("s_wg", gate_w)]
                            if psG_free:
                                waits.append(("s_ve", psG_free))

                            def mmg(t, tc=tc):
                                for e in range(EC):
                                    mm = t.matmul(psG[:, 0:NEXP],
                                                  xln2T[:, e * MYT + tc * 128:e * MYT + (tc + 1) * 128],
                                                  gate_sb[:, e * NEXP:(e + 1) * NEXP],
                                                  start=(e == 0), stop=(e == EC - 1))
                                return mm
                            gmm = S.op("tensor", mmg, waits=waits, inc="s_pe")
                            sc = sc4
                            g0 = S.op("vector", lambda v: v.tensor_copy(sc[:, 0:16], psG[:, 0:16]),
                                      waits=[("s_pe", gmm)], inc="s_ve")
                            psG_free = g0
                            g1 = S.op("vector", lambda v: v.tensor_reduce(
                                t4[:, 0:1], sc[:, 0:16], AX, ALU.max, negate=True),
                                waits=[("s_ve", g0)], inc="s_ve")
                            g2x = S.op("scalar", lambda s: s.activation(
                                sc[:, 16:32], sc[:, 0:16], AF.Exp, bias=t4[:, 0:1],
                                accum_out=t4[:, 1:2]),
                                waits=[("s_ve", g1)], inc="s_ac")
                            g3 = S.op("vector", lambda v: v.reciprocal(t4[:, 2:3], t4[:, 1:2]),
                                      waits=[("s_ac", g2x)], inc="s_ve")
                            g4 = S.op("vector", lambda v: v.tensor_scalar(
                                sc[:, 32:48], sc[:, 16:32], t4[:, 2:3], None, ALU.mult),
                                waits=[("s_ve", g3)], inc="s_ve")
                            g5 = S.op("vector", lambda v: v.tensor_copy(sc[:, 48:64], sc[:, 32:48]),
                                      waits=[("s_ve", g4)], inc="s_ve")
                            prev = g5
                            for it in range(4):
                                m1 = S.op("vector", lambda v: v.tensor_reduce(
                                    t4[:, 3:4], sc[:, 48:64], AX, ALU.max),
                                    waits=[("s_ve", prev)], inc="s_ve")
                                if it < 3:
                                    e1 = S.op("vector", lambda v: v.tensor_scalar(
                                        sc[:, 16:32], sc[:, 48:64], t4[:, 3:4], None, ALU.is_ge),
                                        waits=[("s_ve", m1)], inc="s_ve")
                                    prev = S.op("vector", lambda v: v.scalar_tensor_tensor(
                                        sc[:, 48:64], sc[:, 16:32], -1e30, sc[:, 48:64],
                                        ALU.mult, ALU.add),
                                        waits=[("s_ve", e1)], inc="s_ve")
                                else:
                                    prev = m1
                            ge4 = S.op("vector", lambda v: v.tensor_scalar(
                                sc[:, 16:32], sc[:, 32:48], t4[:, 3:4], None, ALU.is_ge),
                                waits=[("s_ve", prev)], inc="s_ve")
                            S.op("vector", lambda v, tc=tc: v.tensor_mul(
                                w_all[:, tc * NEXP:(tc + 1) * NEXP], sc[:, 32:48], sc[:, 16:32]),
                                waits=[("s_ve", ge4)], inc="s_ve")

                    out_init = S.cnt["s_out"]
                    p3_tr = S.cnt["s_tr"]
                    p4_ve = S.cnt["s_ve"]
                    p4_ac = S.cnt["s_ac"]
                    S.replay(block)

                with contextlib.ExitStack() as c5:
                    sb5 = lambda n, sh, d_: c5.enter_context(nc.sbuf_tensor(n, sh, d_))
                    ps5 = lambda n, sh: c5.enter_context(nc.psum_tensor(n, sh, F32))
                    dn_sb = sb5("dn_sb", [128, FCH * E], BF16)
                    h_sb = sb5("h_sb", [128, FCH * MYT], BF16)
                    gu_ring = [sb5(f"gur{i}", [128, 2 * EC * 128], BF16) for i in range(2)]
                    ystage = [sb5(f"yst{i}", [128, E], F32) for i in range(2)]
                    sg = [sb5(f"sg{i}", [128, 512], BF16) for i in range(2)]
                    psGU = [ps5(f"psGU{i}", [128, 512]) for i in range(4)]
                    psDN = [ps5(f"psDN{i}", [128, 512]) for i in range(4)]

                    for _eng in ("tensor", "vector", "scalar"):
                        S.op(_eng, lambda e: None,
                             waits=[("s_out", out_init), ("s_tr", p3_tr),
                                    ("s_ac", p4_ac), ("s_xm", S.cnt["s_xm"]),
                                    ("s_kvr", S.cnt["s_kvr"])])
                    dn_free = 0
                    ring_free = [0, 0]
                    gu_free = {}
                    dnps_free = {}
                    sg_free = [0, 0]
                    yst_free = [0, 0]
                    gu_uses = 0

                    p5_fence = [("s_out", out_init), ("s_ve", p4_ve),
                                ("s_tr", p3_tr), ("s_ac", p4_ac),
                                ("s_xm", S.cnt["s_xm"]), ("s_kvr", S.cnt["s_kvr"])]
                    pair_recs = {}

                    def gsrc_of(xp2):
                        return guw_d[xp2] if xp2 < NEXP else sgu_d[xp2 - NEXP]

                    def emit_pair(xp2, fi, extra=()):
                        slot = (xp2 * FCH + fi) % 2
                        waits = list(extra)
                        if ring_free[slot]:
                            waits.append(("s_pe", ring_free[slot]))
                        pair_recs[(xp2, fi)] = S.op(
                            "gpsimd", lambda g, xp2=xp2, fi=fi, slot=slot: g.dma_start(
                                out=gu_ring[slot][:].rearrange("p (a e c) -> p a e c", a=2, e=EC),
                                in_=gsrc_of(xp2)[fi]),
                            waits=waits, inc="s_fw", inc_n=16)

                    emit_pair(0, 0, p5_fence)
                    emit_pair(0, 1, p5_fence)
                    for xp in range(n_exp):
                        acc_base = S.cnt["s_acc"]
                        shared = xp >= NEXP
                        dsrc = dnw_d[xp] if not shared else sdn_d[xp - NEXP]
                        waits = [("s_out", out_init)]
                        if xp == 0:
                            waits = list(p5_fence)
                        if dn_free:
                            waits.append(("s_pe", dn_free))
                        dn_rec = S.op("gpsimd", lambda g, dsrc=dsrc: g.dma_start(
                            out=dn_sb[:].rearrange("p (f c) -> p f c", f=FCH), in_=dsrc.rearrange("f p c -> p f c")),
                            waits=waits, inc="s_dn", inc_n=16)

                        h_writes = 0
                        for fi in range(FCH):
                            slot = (xp * FCH + fi) % 2
                            pair_rec = pair_recs[(xp, fi)]
                            last_mm = 0
                            for tg in range(4):
                                gb = (tg % 2) * 2
                                ub = gb + 1
                                for half in range(2):
                                    bank = gb + half
                                    waits = []
                                    if tg == 0 and half == 0:
                                        waits.append(("s_fw", pair_rec))
                                        if fi == 0 and xp == 0:
                                            waits.append(("s_ac", p4_ac))
                                    if gu_free.get(bank):
                                        waits.append(gu_free[bank])

                                    def mmgu(t, slot=slot, half=half, tg=tg, bank=bank):
                                        for e in range(EC):
                                            mm = t.matmul(
                                                psGU[bank][:],
                                                gu_ring[slot][:, half * 2048 + e * 128:half * 2048 + (e + 1) * 128],
                                                xln2T[:, e * MYT + tg * 512:e * MYT + (tg + 1) * 512],
                                                start=(e == 0), stop=(e == EC - 1))
                                        return mm
                                    last_mm = S.op("tensor", mmgu, waits=waits, inc="s_pe")
                                    if half == 0:
                                        waits = [("s_pe", last_mm)]
                                        if sg_free[tg % 2]:
                                            waits.append(("s_ve", sg_free[tg % 2]))
                                        sil = S.op("scalar", lambda s, tg=tg, gb=gb: s.activation(
                                            sg[tg % 2][:], psGU[gb][:], AF.Silu),
                                            waits=waits, inc="s_ac")
                                        gu_free[gb] = ("s_ac", sil)
                                    else:
                                        hv = S.op("vector", lambda v, fi=fi, tg=tg, ub=ub: v.tensor_mul(
                                            h_sb[:, fi * MYT + tg * 512:fi * MYT + (tg + 1) * 512],
                                            sg[tg % 2][:], psGU[ub][:]),
                                            waits=[("s_ac", sil), ("s_pe", last_mm)], inc="s_ve")
                                        gu_free[ub] = ("s_ve", hv)
                                        sg_free[tg % 2] = hv
                                        h_writes = hv
                            ring_free[slot] = last_mm
                            if fi + 2 < FCH:
                                emit_pair(xp, fi + 2)

                        if xp + 1 < n_exp:
                            emit_pair(xp + 1, 0)
                            emit_pair(xp + 1, 1)

                        dn_last = 0
                        for tc2 in range(MTC):
                            for oc in range(4):
                                bank = oc
                                waits = []
                                if tc2 == 0 and oc == 0:
                                    waits += [("s_ve", h_writes), ("s_dn", dn_rec)]
                                if dnps_free.get(bank):
                                    waits.append(("s_ve", dnps_free[bank]))

                                def mmdn(t, tc2=tc2, oc=oc, bank=bank):
                                    for fi in range(FCH):
                                        mm = t.matmul(
                                            psDN[bank][:],
                                            h_sb[:, fi * MYT + tc2 * 128:fi * MYT + (tc2 + 1) * 128],
                                            dn_sb[:, fi * E + oc * 512:fi * E + (oc + 1) * 512],
                                            start=(fi == 0), stop=(fi == FCH - 1))
                                    return mm
                                dmm = S.op("tensor", mmdn, waits=waits, inc="s_pe")
                                dn_last = dmm
                                waits = [("s_pe", dmm)]
                                if oc == 0 and yst_free[tc2 % 2]:
                                    waits.append(("s_acc", yst_free[tc2 % 2]))
                                if not shared:
                                    ystv = S.op("vector", lambda v, tc2=tc2, oc=oc, bank=bank, xp=xp:
                                                v.tensor_scalar(
                                                    ystage[tc2 % 2][:, oc * 512:(oc + 1) * 512],
                                                    psDN[bank][:],
                                                    w_all[:, tc2 * NEXP + xp:tc2 * NEXP + xp + 1],
                                                    None, ALU.mult),
                                                waits=[*waits], inc="s_ve")
                                else:
                                    ystv = S.op("vector", lambda v, tc2=tc2, oc=oc, bank=bank:
                                                v.tensor_copy(ystage[tc2 % 2][:, oc * 512:(oc + 1) * 512],
                                                              psDN[bank][:]),
                                                waits=[*waits], inc="s_ve")
                                dnps_free[bank] = ystv
                            yst_free[tc2 % 2] = S.op("gpsimd", lambda g, tc2=tc2: g.dma_start(
                                out=out[tc2 * 128:(tc2 + 1) * 128, :], in_=ystage[tc2 % 2][:],
                                accum_op=ALU.add),
                                waits=[("s_ve", S.cnt["s_ve"]),
                                       ("s_out", out_init), ("s_acc", acc_base)],
                                inc="s_acc", inc_n=16)
                        dn_free = dn_last

                    S.op("gpsimd", lambda g: None,
                         waits=[("s_acc", S.cnt["s_acc"]), ("s_out", S.cnt["s_out"]),
                                ("s_dmp", S.cnt["s_dmp"])])
                    S.replay(block)

    return nc


def _host_prep(L0_attn_x, L0_attn_Wq, L0_attn_bq, L0_attn_Wk, L0_attn_bk,
               L0_attn_Wv, L0_attn_bv, L0_attn_Wo, L0_attn_bo,
               L0_ffn_gate_weight, L0_ffn_gate_up_weights, L0_ffn_down_weights,
               L0_ffn_shared_gate_up, L0_ffn_shared_down):
    C = np.ascontiguousarray
    x = np.asarray(L0_attn_x, dtype=np.float32)
    Wq = np.asarray(L0_attn_Wq, dtype=np.float32)
    Wk = np.asarray(L0_attn_Wk, dtype=np.float32)
    Wv = np.asarray(L0_attn_Wv, dtype=np.float32)
    Wo = np.asarray(L0_attn_Wo, dtype=np.float32)
    bq = np.asarray(L0_attn_bq, dtype=np.float32)
    bk = np.asarray(L0_attn_bk, dtype=np.float32)
    bv = np.asarray(L0_attn_bv, dtype=np.float32)
    bo = np.asarray(L0_attn_bo, dtype=np.float32)
    gw = np.asarray(L0_ffn_gate_weight, dtype=np.float32)
    guw = np.asarray(L0_ffn_gate_up_weights, dtype=np.float32)
    dnw = np.asarray(L0_ffn_down_weights, dtype=np.float32)
    sgu = np.asarray(L0_ffn_shared_gate_up, dtype=np.float32)
    sdn = np.asarray(L0_ffn_shared_down, dtype=np.float32)

    j = np.arange(64, dtype=np.float64)
    theta = 10000.0 ** (-2.0 * j / D)
    tpos = np.arange(1, T + 1, dtype=np.float64)
    ang = tpos[None, :] * np.concatenate([theta, theta])[:, None]
    cosT = C(np.cos(ang).astype(np.float32))
    sinS = np.sin(ang).astype(np.float32)
    sinS[0:64, :] *= -1.0
    sinS = C(sinS)
    ii = np.arange(128)
    maskT = C(np.where(ii[:, None] <= ii[None, :], 0.0, NMASK).astype(np.float32))
    import ml_dtypes
    identbf = np.eye(128, dtype=np.float32).astype(mybir.dt.np(BF16))

    per_par = {}
    for par in range(2):
        hsl = slice(par * HPC * 128, (par + 1) * HPC * 128)
        wq_t = C(Wq.T[:, hsl].reshape(EC, 128, HPC, 128).transpose(1, 0, 2, 3))
        wk_t = C(Wk.T[:, hsl].reshape(EC, 128, HPC, 128).transpose(1, 0, 2, 3))
        wv_t = C(Wv.T[:, hsl].reshape(EC, 128, HPC * 128).transpose(1, 0, 2))
        bq_c = C(bq[hsl].reshape(HPC, 128).T)
        bk_c = C(bk[hsl].reshape(HPC, 128).T)
        bv_r = C(bv[hsl][None, :])
        per_par[par] = dict(wq=wq_t, wk=wk_t, wvT=wv_t, bq_c=bq_c, bk_c=bk_c, bv_r=bv_r,
                            par0=np.full((128, 1), 1.0 - par, np.float32),
                            par1=np.full((128, 1), float(par), np.float32))

    woT = C(Wo.T.reshape(EC, 128, E))
    bo_r = C(bo[None, :])
    gatewT = C(gw.T.reshape(EC, 128, NEXP).transpose(1, 0, 2))
    guw_t = C(guw.reshape(NEXP, 2, FCH, 128, EC, 128).transpose(0, 2, 5, 1, 4, 3))
    dnw_t = C(dnw.transpose(0, 2, 1).reshape(NEXP, FCH, 128, E))
    g_all = sgu[0:2816]
    u_all = sgu[2816:]
    sgu_t = np.empty((2, FCH, 2, EC, 128, 128), np.float32)
    for X in range(2):
        for fi in range(FCH):
            gt = g_all[(X * FCH + fi) * 128:(X * FCH + fi + 1) * 128]
            ut = u_all[(X * FCH + fi) * 128:(X * FCH + fi + 1) * 128]
            sgu_t[X, fi, 0] = gt.T.reshape(EC, 128, 128)
            sgu_t[X, fi, 1] = ut.T.reshape(EC, 128, 128)
    sgu_t = C(sgu_t.transpose(0, 1, 4, 2, 3, 5))
    sdn_t = C(sdn.T.reshape(2, FCH, 128, E))

    shared_all = dict(woT=woT, bo_r=bo_r, cosT=cosT, sinS=sinS,
                      maskT=maskT, identbf=identbf, gatewT=gatewT, guw=guw_t,
                      dnw=dnw_t, sguw=sgu_t, sdnw=sdn_t)
    in_maps = []
    for core in range(8):
        b = core // 2
        par = core % 2
        m = dict(shared_all)
        m.update(per_par[par])
        m["xfull"] = C(x[b])
        m["xmine"] = C(x[b, par * MYT:(par + 1) * MYT])
        in_maps.append(m)
    return in_maps, x.dtype


def kernel(**inputs):
    in_maps, xdt = _host_prep(**inputs)
    nc = build_program()
    res = run_bass_kernel_spmd(nc, in_maps, core_ids=list(range(8)))
    outf = np.empty((B, T, E), dtype=np.float32)
    for core in range(8):
        b, par = core // 2, core % 2
        outf[b, par * MYT:(par + 1) * MYT] = res.results[core]["out"]
    return outf
